# version 30
# baseline (speedup 1.0000x reference)
import contextlib

import numpy as np

import concourse.bass as bass
import concourse.mybir as mybir
from concourse.bass_utils import run_bass_kernel_spmd

dt = mybir.dt
F32, BF16, F16, FP8 = dt.float32, dt.bfloat16, dt.float16, dt.float8e4
AF = mybir.ActivationFunctionType
OP = mybir.AluOpType
AX = mybir.AxisListType

NCORES = 8
E = 1024
H = 16
HD = 64
HID = 4096
NCH = E // 128
NHC = HID // 128
EPS = 1e-5
FP8_MAX = 448.0
ZPAD = 128

_T = 2048
_DEBUG_TAPS = ()



class Buf:
    __slots__ = ("t", "w", "readers", "name")

    def __init__(self, tile, name=""):
        self.t = tile
        self.w = None
        self.readers = []
        self.name = name

    def __getitem__(self, key):
        return self.t[key]

    @property
    def ap(self):
        return self.t[:]


class Bld:
    def __init__(self, T):
        self.T = T
        self.nc = bass.Bass("TRN2", target_bir_lowering=False)
        self.es = contextlib.ExitStack()
        nc = self.nc
        self.eng = {"tensor": nc.tensor, "vector": nc.vector,
                    "scalar": nc.scalar, "gpsimd": nc.gpsimd, "sync": nc.sync}
        self.sems = {}
        self.cnt = {}
        for name in ("mT", "mV", "mS", "mG"):
            self.sems[name] = self.es.enter_context(nc.semaphore(name))
            self.cnt[name] = 0
        self.marksem = {"tensor": "mT", "vector": "mV", "scalar": "mS",
                        "gpsimd": "mG"}
        self.waited = {e: {} for e in self.eng}
        self._konst_tile = None
        self._konst_map = {}
        self.phases = []
        self._seg_free = []
        self._seg_used = {}
        self._seg_tomb = []
        self._arena_lo = None
        self._arena_hi = None

    def phase(self, name):
        self.phases.append((name, self.nc.n_instructions()))

    def konst(self, value):
        if self._konst_tile is None:
            self._konst_tile = Buf(self.nc.alloc_sbuf_tensor(
                "konst", [128, 32], F32), "konst")
        if value not in self._konst_map:
            i = len(self._konst_map)
            assert i < 32
            self._konst_map[value] = i
            self.op("vector",
                    lambda v, i=i: v.memset(
                        self._konst_tile.t[:, i:i + 1], float(value)),
                    writes=[self._konst_tile])
        return self._konst_tile.t[:, self._konst_map[value]:
                                  self._konst_map[value] + 1]

    @property
    def kbuf(self):
        return self._konst_tile

    def sb(self, name, shape, dty):
        return Buf(self.nc.alloc_sbuf_tensor(name, shape, dty), name)

    def ps(self, name, shape, dty=F32):
        return Buf(self.nc.alloc_psum_tensor(name, shape, dty), name)

    def dram(self, name, shape, dty, kind="Internal"):
        return self.nc.dram_tensor(name, shape, dty, kind=kind)

    def _arena_init(self):
        if self._arena_lo is not None:
            return
        self.konst(1.0)
        lo = self.nc.sbuf_base
        hi = self.nc.sbuf_top
        self._arena_lo, self._arena_hi = lo, hi
        self._seg_free = [[lo, hi]]
        self.nc.sbuf_base = hi

    def aalloc(self, name, shape, dty):
        self._arena_init()
        nbytes = int(np.prod(shape[1:])) * {F32: 4, BF16: 2, F16: 2,
                                            FP8: 1}[dty]
        nbytes = (nbytes + 31) // 32 * 32
        for seg in self._seg_free:
            if seg[1] - seg[0] >= nbytes:
                off = seg[0]
                seg[0] += nbytes
                tile = self.nc.alloc_sbuf_tensor_at(
                    f"{name}@{off}", shape, dty, offset=off)
                buf = Buf(tile, name)
                for (s, e, ticks) in self._seg_tomb:
                    if s < off + nbytes and e > off:
                        buf.readers.extend(ticks)
                self._seg_used[name] = (off, off + nbytes, buf)
                return buf
        raise MemoryError(
            f"arena: no room for {name} ({nbytes}B/partition); "
            f"free={[(s, e - s) for s, e in self._seg_free]}")

    def afree(self, *names):
        for name in names:
            off, end, buf = self._seg_used.pop(name)
            ticks = list(buf.readers)
            if buf.w is not None:
                ticks.append(buf.w)
            self._seg_tomb = [(s, e, tk) for (s, e, tk) in self._seg_tomb
                              if not (s >= off and e <= end)]
            self._seg_tomb.append((off, end, ticks))
            self._seg_free.append([off, end])
        self._seg_free.sort()
        out = []
        for seg in self._seg_free:
            if out and out[-1][1] == seg[0]:
                out[-1][1] = seg[1]
            else:
                out.append(seg)
        self._seg_free = out

    def _mark(self, instr, sem, inc):
        instr.then_inc(self.sems[sem], inc)
        self.cnt[sem] += inc
        return (sem, self.cnt[sem])

    def chan(self, ch):
        name = "D:" + ch
        if name not in self.sems:
            self.sems[name] = self.es.enter_context(self.nc.semaphore(name))
            self.cnt[name] = 0
        return name

    _CH_DEPTH = 2

    def _subchan(self, ch):
        key = "R:" + ch
        i = self.cnt.get(key, 0)
        self.cnt[key] = (i + 1) % self._CH_DEPTH
        return self.chan(f"{ch}.{i}")

    def need(self, eng, ticket):
        if ticket is None:
            return
        sem, val = ticket
        if sem.startswith("D:"):
            val = self.cnt[sem]
        w = self.waited[eng]
        if w.get(sem, 0) < val:
            self.eng[eng].wait_ge(self.sems[sem], val)
            w[sem] = val

    def _pre(self, eng, reads, writes):
        for b in reads:
            if b is not None and b.w is not None:
                self.need(eng, b.w)
        for b in writes:
            if b is None:
                continue
            if b.w is not None:
                self.need(eng, b.w)
            for t in b.readers:
                self.need(eng, t)

    def _post(self, tick, reads, writes):
        for b in reads:
            if b is not None:
                b.readers.append(tick)
        for b in writes:
            if b is not None:
                b.w = tick
                b.readers = []
        return tick

    def op(self, eng, fn, reads=(), writes=()):
        self._pre(eng, reads, writes)
        instr = fn(self.eng[eng])
        tick = self._mark(instr, self.marksem[eng], 1)
        return self._post(tick, reads, writes)

    def dma(self, eng, out_ap, in_ap, ch="ld", reads=(), writes=()):
        self._pre(eng, reads, writes)
        chn = self._subchan(ch)
        self.need(eng, (chn, self.cnt[chn]))
        instr = self.eng[eng].dma_start(out=out_ap, in_=in_ap)
        tick = self._mark(instr, chn, 16)
        return self._post(tick, reads, writes)

    def mm_group(self, fns, reads=(), writes=()):
        self._pre("tensor", reads, writes)
        instr = None
        for fn in fns:
            instr = fn(self.nc.tensor)
        tick = self._mark(instr, "mT", 1)
        return self._post(tick, reads, writes)

    def rebind(self, buf, name=""):
        nb = Buf(buf.t, name or buf.name)
        nb.w = buf.w
        nb.readers = list(buf.readers)
        return nb

    def tap(self, name, bufs, rows=128, cols=None):
        if name not in _DEBUG_TAPS:
            return
        n = len(bufs)
        w = cols or bufs[0].t.shape[-1]
        d = self.dram(f"dbg_{name}", [n, rows, w], bufs[0].t.dtype,
                      kind="ExternalOutput")
        for i, bb in enumerate(bufs):
            self.dma("scalar", d[i, :rows, :], bb.t[:rows, :w], ch="dbg",
                     reads=[bb])



def build_program(T=None, n_cores=NCORES):
    T = T or _T
    NT = T // 128
    QSP = min(512, T)
    NQS = T // QSP
    TCH = min(512, T // 4)
    NTC = T // TCH
    THL = min(1024, T)
    PW = H * (HD + 1)
    ELEN = ZPAD + T

    b = Bld(T)
    nc = b.nc
    K = b.konst

    DI = {}

    def din(name, shape, dty=F32):
        DI[name] = b.dram(name, shape, dty, kind="ExternalInput").ap()
        return DI[name]

    x_in = din("xT", [E, T])
    din("l0_relstrip", [H, 128, T])
    din("l0_qkT", [E, 2 * E])
    din("l0_bqk_c", [128, 16])
    din("l0_vT_ext", [E, PW])
    din("l0_bv_ext", [1, PW])
    din("l0_outT", [E, E])
    din("l0_out_b_c", [128, NCH])
    din("l0_w1T", [E, HID])
    din("l0_b1_c", [128, NHC])
    din("l0_w2T", [HID, E])
    din("l0_b2_c", [128, NCH])
    din("l0_ln1_g_c", [128, NCH])
    din("l0_ln1_b_c", [128, NCH])
    din("l0_ln2_g_c", [128, NCH])
    din("l0_ln2_b_c", [128, NCH])
    din("l0_ffnT", [E, E])
    din("l0_ffn_b_c", [128, NCH])
    din("l1_WqT", [E, E])
    din("l1_bq_c", [128, NCH])
    din("l1_WkT", [E, HD])
    din("l1_vT_ext", [E, HD + 1])
    din("l1_bv_ext", [1, HD + 1])
    din("l1_WoT", [E, E])
    din("l1_bo_c", [128, NCH])
    din("l1_w1T", [E, HID])
    din("l1_convdiag", [NHC, 128, 3 * 128])
    din("l1_conv_b_c", [128, NHC])
    din("l1_w2T", [HID, E])

    out_d = b.dram("outT", [E, T], F32, kind="ExternalOutput").ap()

    xb_spill = b.dram("xb_spill", [E, T], F32).ap()
    x1_spill = b.dram("x1_spill", [E, T], F32).ap()
    xq8_spill = b.dram("xq8_spill", [E, T], FP8).ap()
    xq1_spill = b.dram("xq1_spill", [E, T], BF16).ap()
    h2_spill = b.dram("h2_spill", [HID, T], F32).ap()
    w2q_spill = b.dram("w2q_spill", [NCH * 128, NHC * 128], BF16).ap()
    cc_in_d = b.dram("cc_in", [1, 8], F32)
    cc_out_d = b.dram("cc_out", [1, 8], F32)
    WB = {}
    for nm, shp in (("l0_qkT", [E, 2 * E]), ("l0_vT_ext", [E, PW]),
                    ("l0_outT", [E, E]), ("l0_w1T", [E, HID]),
                    ("l0_w2T", [HID, E]), ("l1_WqT", [E, E]),
                    ("l1_WkT", [E, HD]), ("l1_vT_ext", [E, HD + 1]),
                    ("l1_WoT", [E, E]),
                    ("l1_convdiag", [NHC, 128, 3 * 128]),
                    ("l0_relstrip", [H, 128, T])):
        WB[nm] = b.dram(nm + "_bf", shp, BF16).ap()
    bounce_d = b.dram("bounce", [1, 64], F32).ap()
    stok_d = b.dram("stok_d", [2, T], F32).ap()
    sig_d = b.dram("sig_d", [H, T], F32).ap()
    sig2_d = b.dram("sig2_d", [1, H * T], BF16).ap()
    dbu = {nm: Buf(None, nm) for nm in
           ("xb", "x1", "xq8", "xq1", "h2", "w2q", "bounce", "stok",
            "sig0", "sig1")}

    c8 = [b.sb(f"c{c}", [128, T], BF16) for c in range(NCH)]
    ctx8 = [None] * NCH

    def ctx8_alloc():
        for c in range(NCH):
            ctx8[c] = b.aalloc(f"ctx{c}", [128, T], BF16)

    def ctx8_free():
        b.afree(*[f"ctx{c}" for c in range(NCH)])
    ones_bf = b.sb("ones_bf", [128, 128], BF16)
    id32 = b.sb("id32", [128, 128], F32)
    id16 = b.sb("id16", [128, 128], BF16)
    ft = b.sb("ft", [128, T], F32)
    aa = bbt = None

    def ln_alloc():
        nonlocal aa, bbt
        aa = b.aalloc("aa", [128, T], F32)
        bbt = b.aalloc("bbt", [128, T], F32)

    def ln_free():
        b.afree("aa", "bbt")
    tmp = [b.sb(f"tmp{i}", [128, 1024], F32) for i in range(2)]
    wb = [b.sb(f"wb{i}", [128, 1024], BF16) for i in range(2)]
    smalls = {}
    for nm, w_ in (("g1", NCH), ("b1", NCH), ("g2", NCH), ("b2", NCH),
                   ("ob", NCH), ("fb", NCH), ("bo1", NCH), ("bq1", NCH),
                   ("bq1s", NCH), ("mb1", NHC), ("mb2", NCH), ("cb", NHC),
                   ("bqk", 16), ("bqks", 8), ("negc", 16), ("sc", 32)):
        smalls[nm] = b.sb(nm, [128, w_], F32)
    amax_c = smalls["sc"]

    pA = b.ps("pA", [128, 1024])
    pB = b.ps("pB", [128, 1024])
    pC = b.ps("pC", [128, 1024])
    pD = b.ps("pD", [128, 1024])

    def blocks(total, step=512):
        return [(o, min(step, total - o)) for o in range(0, total, step)]

    b.op("vector", lambda v: v.memset(ones_bf.ap, 1.0), writes=[ones_bf])
    b.op("vector", lambda v: v.memset(id32.ap, 1.0), writes=[id32])
    b.op("gpsimd", lambda g: g.affine_select(
        id32.ap, id32.ap, [[1, 128]], OP.is_equal, 0.0, base=0,
        channel_multiplier=-1), reads=[id32], writes=[id32])
    b.op("vector", lambda v: v.tensor_copy(id16.ap, id32.ap),
         reads=[id32], writes=[id16])
    def emit_smalls():
        for nm, key in (("l0_ln1_g_c", "g1"), ("l0_ln1_b_c", "b1"),
                        ("l0_ln2_g_c", "g2"), ("l0_ln2_b_c", "b2"),
                        ("l0_out_b_c", "ob"), ("l0_ffn_b_c", "fb"),
                        ("l1_bo_c", "bo1"), ("l1_bq_c", "bq1"),
                        ("l0_b1_c", "mb1"), ("l0_b2_c", "mb2"),
                        ("l1_conv_b_c", "cb"), ("l0_bqk_c", "bqk")):
            b.dma("sync", smalls[key].ap, DI[nm][:, :], ch="const",
                  writes=[smalls[key]])
        b.op("vector", lambda v: v.tensor_scalar(
            smalls["bq1s"].ap, smalls["bq1"].ap, 0.125, None, OP.mult),
            reads=[smalls["bq1"]], writes=[smalls["bq1s"]])
        b.op("vector", lambda v: v.tensor_scalar(
            smalls["bqks"].ap, smalls["bqk"].t[:, 0:8], 0.125, None,
            OP.mult), reads=[smalls["bqk"]], writes=[smalls["bqks"]])

    dbu_wbt = {}

    def emit_precasts():
        for nm in ("l0_vT_ext", "l0_qkT", "l0_relstrip", "l0_outT",
                   "l0_w1T", "l0_w2T", "l1_WqT", "l1_WkT", "l1_vT_ext",
                   "l1_WoT", "l1_convdiag"):
            t_ = DI[nm].tensor
            tot = 1
            for d_ in t_.shape:
                tot *= int(d_)
            dbu_wbt[nm] = Buf(None, nm + "cast")
            step = 2 ** 20
            for o0 in range(0, tot, step):
                ow = min(step, tot - o0)
                b.dma("gpsimd",
                      bass.AP(WB[nm].tensor, o0, [[1, 1], [1, ow]]),
                      bass.AP(t_, o0, [[1, 1], [1, ow]]),
                      ch="wc" + str((o0 // step) % 2),
                      writes=[dbu_wbt[nm]])

    def ln_stats_from_c8():
        sqb = [wb[0], wb[1]]
        pos = [(pA, 0), (pA, 512), (pB, 0), (pB, 512)]
        pos2 = [(pC, 0), (pC, 512), (pD, 0), (pD, 512)]
        for c in range(NCH):
            for tc in range(NTC):
                sl = slice(tc * TCH, (tc + 1) * TCH)
                ph, po = pos[tc % 4]
                b.mm_group([lambda t, c=c, sl=sl, ph=ph, po=po: t.matmul(
                    ph.t[:, po:po + TCH], ones_bf.ap, c8[c].t[:, sl],
                    start=(c == 0), stop=(c == NCH - 1))],
                    reads=[ones_bf, c8[c]], writes=[ph])
                s_ = sqb[(c * NTC + tc) % 2]
                b.op("scalar", lambda s, c=c, sl=sl, s_=s_: s.activation(
                    s_.t[:, 0:TCH], c8[c].t[:, sl], AF.Square),
                    reads=[c8[c]], writes=[s_])
                ph2, po2 = pos2[tc % 4]
                b.mm_group([lambda t, s_=s_, ph2=ph2, po2=po2: t.matmul(
                    ph2.t[:, po2:po2 + TCH], ones_bf.ap, s_.t[:, 0:TCH],
                    start=(c == 0), stop=(c == NCH - 1))],
                    reads=[ones_bf, s_], writes=[ph2])
        for tc in range(NTC):
            sl = slice(tc * TCH, (tc + 1) * TCH)
            ph, po = pos[tc % 4]
            ph2, po2 = pos2[tc % 4]
            b.op("vector", lambda v, sl=sl, ph=ph, po=po: v.tensor_scalar(
                ft.t[:, sl], ph.t[:, po:po + TCH], 1.0 / E, None,
                OP.mult), reads=[ph], writes=[ft])
            b.op("vector", lambda v, sl=sl: v.scalar_tensor_tensor(
                tmp[1].t[:, 0:TCH], ft.t[:, sl], -1.0,
                ft.t[:, sl], OP.mult, OP.mult),
                reads=[ft], writes=[tmp[1]])
            b.op("vector", lambda v: v.tensor_scalar(
                tmp[1].t[:, 0:TCH], tmp[1].t[:, 0:TCH], EPS, None, OP.add),
                reads=[tmp[1]], writes=[tmp[1]])
            b.op("vector", lambda v, sl=sl, ph2=ph2, po2=po2:
                 v.scalar_tensor_tensor(
                     aa.t[:, sl], ph2.t[:, po2:po2 + TCH], 1.0 / E,
                     tmp[1].t[:, 0:TCH], OP.mult, OP.add),
                 reads=[ph2, tmp[1]], writes=[aa])
        b.op("scalar", lambda s: s.activation(
            aa.ap, aa.ap, AF.Sqrt), reads=[aa], writes=[aa])
        b.op("vector", lambda v: v.reciprocal(aa.ap, aa.ap),
             reads=[aa], writes=[aa])
        b.op("vector", lambda v: v.scalar_tensor_tensor(
            bbt.ap, ft.ap, -1.0, aa.ap, OP.mult, OP.mult),
            reads=[ft, aa], writes=[bbt])

    def load_wslices(dst, dram_ap, col0, ncols, nin=E, ch="w", cast=True,
                     dst_off=0, ic0=0, nic_w=None):
        nic = nic_w if nic_w is not None else nin // 128
        t_ = dram_ap.tensor
        stride = t_.shape[-1]
        src = bass.AP(t_, col0 + ic0 * 128 * stride,
                      [[stride, 128], [128 * stride, nic], [1, ncols]])
        dfree = dst.t.shape[-1]
        dst_ap = bass.AP(dst.t, dst_off, [[dfree, 128], [ncols, nic],
                                          [1, ncols]])
        reads = ([dbu_wbt[t_.name[:-3]]] if t_.name.endswith("_bf")
                 else [])
        eng = {"wf0": "gpsimd", "wf1": "sync"}.get(ch, "sync")
        return b.dma(eng, dst_ap, src, ch=ch, reads=reads, writes=[dst])

    def part_scalar_reduce(src_col, dst_col, op, bcast=True, bcol=0):
        b.op("tensor", lambda t: t.transpose(
            pD.t[0:1, 0:128], amax_c.t[:, src_col:src_col + 1], id32.ap),
            reads=[amax_c, id32], writes=[pD])
        b.op("vector", lambda v: v.tensor_reduce(
            amax_c.t[0:1, dst_col:dst_col + 1], pD.t[0:1, 0:128], AX.X, op),
            reads=[pD], writes=[amax_c])
        if bcast:
            b.dma("sync", bounce_d[0:1, bcol:bcol + 1],
                  amax_c.t[0:1, dst_col:dst_col + 1], ch="bncw",
                  reads=[amax_c], writes=[dbu["bounce"]])
            b.dma("sync", amax_c.t[:, dst_col:dst_col + 1],
                  bounce_d[0:1, bcol:bcol + 1].broadcast_to((128, 1)),
                  ch="bncr", reads=[dbu["bounce"]], writes=[amax_c])

    def glob_prep(col_in, col_clip, col_recip, col_scale, k_scale, clip_lo):
        b.op("vector", lambda v: v.tensor_scalar(
            amax_c.t[:, col_clip:col_clip + 1],
            amax_c.t[:, col_in:col_in + 1], clip_lo, None, OP.max),
            reads=[amax_c], writes=[amax_c])
        b.op("vector", lambda v: v.reciprocal(
            amax_c.t[:, col_recip:col_recip + 1],
            amax_c.t[:, col_clip:col_clip + 1]),
            reads=[amax_c], writes=[amax_c])
        b.op("vector", lambda v: v.tensor_scalar(
            amax_c.t[:, col_scale:col_scale + 1],
            amax_c.t[:, col_recip:col_recip + 1], k_scale, None, OP.mult),
            reads=[amax_c], writes=[amax_c])

    def featmax_scales(m1, scl_o, rs_o):
        for tt in range(NT):
            b.op("tensor", lambda t, tt=tt: t.transpose(
                pD.t[:, 0:128], m1.t[:, tt * 128:(tt + 1) * 128], id32.ap),
                reads=[m1, id32], writes=[pD])
            b.op("vector", lambda v, tt=tt: v.tensor_reduce(
                tmp[0].t[:, tt:tt + 1], pD.t[:, 0:128], AX.X, OP.max),
                reads=[pD], writes=[tmp[0]])
        b.op("scalar", lambda s: s.activation(
            tmp[0].t[:, 0:NT], tmp[0].t[:, 0:NT], AF.Sqrt),
            reads=[tmp[0]], writes=[tmp[0]])
        b.op("vector", lambda v: v.tensor_scalar(
            tmp[0].t[:, 0:NT], tmp[0].t[:, 0:NT], 1e-5, None, OP.max),
            reads=[tmp[0]], writes=[tmp[0]])
        b.op("vector", lambda v: v.reciprocal(
            tmp[1].t[:, 0:NT], tmp[0].t[:, 0:NT]),
            reads=[tmp[0]], writes=[tmp[1]])
        b.op("vector", lambda v: v.tensor_scalar(
            tmp[1].t[:, 0:NT], tmp[1].t[:, 0:NT], 127.0, None, OP.mult),
            reads=[tmp[1]], writes=[tmp[1]])
        b.op("vector", lambda v: v.tensor_scalar(
            tmp[0].t[:, 0:NT], tmp[0].t[:, 0:NT], 1.0 / 127.0, None,
            OP.mult), reads=[tmp[0]], writes=[tmp[0]])
        b.op("vector", lambda v: v.tensor_copy(
            tmp[1].t[:, NT:2 * NT], tmp[0].t[:, 0:NT]),
            reads=[tmp[0]], writes=[tmp[1]])
        b.op("tensor", lambda t: t.transpose(
            pD.t[0:2 * NT, 0:128], tmp[1].t[:, 0:2 * NT], id32.ap),
            reads=[tmp[1], id32], writes=[pD])
        b.op("vector", lambda v: v.tensor_copy(
            tmp[0].t[0:2 * NT, 0:128], pD.t[0:2 * NT, 0:128]),
            reads=[pD], writes=[tmp[0]])
        b.dma("sync", stok_d[0:1, 0:T].rearrange("a (n x) -> (a n) x",
                                                   x=128),
              tmp[0].t[0:NT, 0:128], ch="stok", reads=[tmp[0]],
              writes=[dbu["stok"]])
        b.dma("sync", stok_d[1:2, 0:T].rearrange("a (n x) -> (a n) x",
                                                   x=128),
              tmp[0].t[NT:2 * NT, 0:128], ch="stok", reads=[tmp[0]],
              writes=[dbu["stok"]])
        b.dma("sync", scl_o.ap, stok_d[0:1, 0:T].broadcast_to((128, T)),
              ch="stok", reads=[dbu["stok"]], writes=[scl_o])
        b.dma("sync", rs_o.ap, stok_d[1:2, 0:T].broadcast_to((128, T)),
              ch="stok", reads=[dbu["stok"]], writes=[rs_o])

    a8 = [b.aalloc(f"a{c}", [128, T], F32) for c in range(NCH)]
    _xeng = ["sync", "scalar", "gpsimd"]
    for c in range(NCH):
        b.dma(_xeng[c % 3], a8[c].ap, x_in[c * 128:(c + 1) * 128, :],
              ch=f"x{c % 3}", writes=[a8[c]])
    emit_smalls()
    b.need("gpsimd", a8[NCH - 2].w)
    b.need("gpsimd", a8[NCH - 1].w)
    emit_precasts()
    for c in range(NCH):
        b.op("vector", lambda v, c=c: v.tensor_copy(c8[c].ap, a8[c].ap),
             reads=[a8[c]], writes=[c8[c]])
    ln_alloc()
    ln_stats_from_c8()
    for c in range(NCH):
        b.op("vector", lambda v, c=c: v.tensor_tensor(
            a8[c].ap, a8[c].ap, aa.ap, OP.mult),
            reads=[a8[c], aa], writes=[a8[c]])
        b.op("vector", lambda v, c=c: v.tensor_tensor(
            a8[c].ap, a8[c].ap, bbt.ap, OP.add),
            reads=[a8[c], bbt], writes=[a8[c]])
    ln_free()
    b.tap("xn0", a8)

    b.phase("amax8")
    for c in range(NCH):
        b.op("vector", lambda v, c=c: v.tensor_reduce(
            tmp[0].t[:, c:c + 1], a8[c].ap, AX.X, OP.max,
            apply_absolute_value=True), reads=[a8[c]], writes=[tmp[0]])
    b.op("vector", lambda v: v.tensor_reduce(
        amax_c.t[:, 0:1], tmp[0].t[:, 0:NCH], AX.X, OP.max),
        reads=[tmp[0]], writes=[amax_c])
    part_scalar_reduce(0, 1, OP.max, bcast=False)
    b.op("vector", lambda v: v.tensor_copy(
        amax_c.t[0:1, 24:32], amax_c.t[0:1, 1:2].broadcast_to((1, 8))),
        reads=[amax_c], writes=[amax_c])
    cc_w = b.dma("sync", cc_in_d.ap()[0:1, 0:8],
                 amax_c.t[0:1, 24:32], ch="cc", reads=[amax_c])
    b.need("gpsimd", cc_w)
    cc_tick = b._mark(nc.gpsimd.collective_compute(
        "AllReduce", OP.max, replica_groups=[list(range(n_cores))],
        ins=[cc_in_d.ap().opt()], outs=[cc_out_d.ap().opt()]), "mG", 1)
    ccbuf = Buf(None, "ccres")
    ccbuf.w = cc_tick
    b.phase("ln1")
    for c in range(NCH):
        b.op("vector", lambda v, c=c: v.tensor_copy(c8[c].ap, a8[c].ap),
             reads=[a8[c]], writes=[c8[c]])
    ln_alloc()
    ln_stats_from_c8()
    for c in range(NCH):
        b.op("vector", lambda v, c=c: v.tensor_tensor(
            ft.ap, a8[c].ap, aa.ap, OP.mult),
            reads=[a8[c], aa], writes=[ft])
        b.op("vector", lambda v, c=c: v.tensor_tensor(
            ft.ap, ft.ap, bbt.ap, OP.add), reads=[ft, bbt], writes=[ft])
        b.op("vector", lambda v, c=c: v.tensor_scalar(
            c8[c].ap, ft.ap, smalls["g1"].t[:, c:c + 1],
            smalls["b1"].t[:, c:c + 1], OP.mult, OP.add),
            reads=[ft, smalls["g1"], smalls["b1"]], writes=[c8[c]])
    ln_free()
    b.tap("xln1", c8)

    b.dma("sync", amax_c.t[:, 3:4],
          cc_out_d.ap()[0:1, 0:1].broadcast_to((128, 1)), ch="cc2",
          reads=[ccbuf], writes=[amax_c])
    glob_prep(3, 4, 5, 6, 224.0, 1e-12)
    q8s = b.aalloc("q8s", [128, T], FP8)
    for c in range(NCH):
        b.op("vector", lambda v, c=c: v.tensor_scalar(
            q8s.ap, a8[c].ap, amax_c.t[:, 6:7], None, OP.mult),
            reads=[a8[c], amax_c], writes=[q8s])
        b.dma("sync", xq8_spill[c * 128:(c + 1) * 128, :], q8s.ap,
              ch="spill", reads=[q8s], writes=[dbu["xq8"]])
    b.afree("q8s")

    for c in range(NCH):
        b.dma("sync", xb_spill[c * 128:(c + 1) * 128, :], a8[c].ap,
              ch="spill", reads=[a8[c]], writes=[dbu["xb"]])
    b.afree(*[f"a{c}" for c in range(NCH)])

    b.phase("Vproj")
    vt = [b.aalloc(f"vt{i}", [128, PW], BF16) for i in range(NT)]
    bvx = b.aalloc("bvx", [128, PW], F32)
    b.dma("sync", bvx.ap, DI["l0_bv_ext"][0:1, :].broadcast_to((128, PW)),
          ch="const", writes=[bvx])
    wvx = b.aalloc("wvx", [128, NCH * PW], BF16)
    load_wslices(wvx, WB["l0_vT_ext"], 0, PW, ch="wv")
    for tt in range(NT):
        fns = []
        for ic in range(NCH):
            for o0, ow in blocks(PW):
                ph, po = (pA, o0) if o0 < 1024 else (pB, o0 - 1024)
                fns.append(lambda t, ic=ic, o0=o0, ow=ow, tt=tt, ph=ph,
                           po=po: t.matmul(
                               ph.t[:, po:po + ow],
                               c8[ic].t[:, tt * 128:(tt + 1) * 128],
                               wvx.t[:, ic * PW + o0: ic * PW + o0 + ow],
                               start=(ic == 0), stop=(ic == NCH - 1)))
        b.mm_group(fns, reads=list(c8) + [wvx], writes=[pA, pB])
        w1_ = min(1024, PW)
        b.op("vector", lambda v, tt=tt, w1_=w1_: v.tensor_tensor(
            vt[tt].t[:, 0:w1_], pA.t[:, 0:w1_], bvx.t[:, 0:w1_], OP.add),
            reads=[pA, bvx], writes=[vt[tt]])
        if PW > 1024:
            b.op("vector", lambda v, tt=tt: v.tensor_tensor(
                vt[tt].t[:, 1024:PW], pB.t[:, 0:PW - 1024],
                bvx.t[:, 1024:PW], OP.add), reads=[pB, bvx],
                writes=[vt[tt]])
    b.afree("wvx")
    b.tap("vt0", [vt[0]])

    def attn_layer(layer, vtiles, vstride, extra=None):
        causal = layer == 0
        qkq = [b.aalloc(f"qkq{i}", [128, T], BF16) for i in range(2)]
        qkk = [b.aalloc(f"qkk{i}", [128, T], BF16) for i in range(2)]
        wqk = [b.aalloc(f"wqk{i}", [128, 2048], BF16) for i in range(2)]
        ptb = [[b.aalloc(f"pt{h}{i}", [128, QSP], BF16) for i in range(4)]
               for h in range(2)]
        stg = [b.aalloc(f"stg{h}", [128, QSP], BF16) for h in range(2)]
        stg2 = b.aalloc("stg2", [128, QSP], BF16)
        s64 = [b.aalloc(f"s64{h}", [128, QSP], F32) for h in range(2)]
        if causal:
            est = [[b.aalloc(f"est{j}{i}", [128, T], BF16)
                    for i in range(2)] for j in range(2)]
        negc = smalls["negc"]
        sS = [[(pA.t, 0, Buf(None, "sA0")), (pA.t, 512, Buf(None, "sA1"))],
              [(pB.t, 0, Buf(None, "sB0")), (pB.t, 512, Buf(None, "sB1"))]]
        cX = [(pC.t, 0, Buf(None, "cxA")), (pC.t, 512, Buf(None, "cxB"))]

        if layer == 1:
            load_wslices(wqk[0], WB["l1_WkT"], 0, HD, ch="wqk0")
            for th, tw in blocks(T):
                fns = [
                    (lambda t, ic=ic, th=th, tw=tw: t.matmul(
                        pD.t[0:64, 0:tw],
                        wqk[0].t[:, ic * HD: ic * HD + HD],
                        c8[ic].t[:, th:th + tw],
                        start=(ic == 0), stop=(ic == NCH - 1)))
                    for ic in range(NCH)]
                b.mm_group(fns, reads=list(c8) + [wqk[0]], writes=[pD])
                b.op("scalar", lambda s, th=th, tw=tw: s.activation(
                    qkk[0].t[0:64, th:th + tw], pD.t[0:64, 0:tw],
                    AF.Identity), reads=[pD], writes=[qkk[0]])
            b.dma("sync", qkk[0].t[64:128, 0:T], qkk[0].t[0:64, 0:T],
                  ch="ctxmv", reads=[qkk[0]], writes=[qkk[0]])

        def pair_tasks(p):
            par = p % 2
            hA, hB = 2 * p, 2 * p + 1
            tasks = []
            if layer == 0:
                tasks.append(lambda: load_wslices(
                    wqk[par], WB["l0_qkT"], p * 128, 128,
                    ch=f"wqk{par}"))
                tasks.append(lambda: load_wslices(
                    wqk[par], WB["l0_qkT"], E + p * 128, 128,
                    ch=f"wqk{par}", dst_off=1024))
                for jj, hh in ((0, hA), (1, hB)):
                    def ldslab(hh=hh, jj=jj):
                        b.dma("sync", est[jj][par].ap,
                              WB["l0_relstrip"][hh, :, :],
                              ch=f"estl{jj}",
                              reads=[dbu_wbt["l0_relstrip"]],
                              writes=[est[jj][par]])
                    tasks.append(ldslab)
            else:
                tasks.append(lambda: load_wslices(
                    wqk[par], WB["l1_WqT"], p * 128, 128, ch=f"wqk{par}"))

            def proj_block(dst, w_off, th, tw, bias_ap, bias_bufs,
                           scale_ap, scale_bufs):
                def go():
                    fns = [
                        (lambda t, ic=ic: t.matmul(
                            pD.t[:, 0:tw],
                            wqk[par].t[:, w_off + ic * 128:
                                       w_off + ic * 128 + 128],
                            c8[ic].t[:, th:th + tw],
                            start=(ic == 0), stop=(ic == NCH - 1)))
                        for ic in range(NCH)]
                    b.mm_group(fns, reads=list(c8) + [wqk[par]],
                               writes=[pD])
                    b.op("scalar", lambda s: s.activation(
                        dst.t[:, th:th + tw], pD.t[:, 0:tw],
                        AF.Identity, bias=bias_ap, scale=scale_ap),
                        reads=[pD] + bias_bufs + scale_bufs, writes=[dst])
                return go

            qsc = K(0.125)
            qbias = (smalls["bqks"].t[:, p:p + 1] if layer == 0
                     else smalls["bq1s"].t[:, p:p + 1])
            qbb = [smalls["bqks" if layer == 0 else "bq1s"]]
            for th, tw in blocks(T):
                tasks.append(proj_block(qkq[par], 0, th, tw, qbias, qbb,
                                        qsc, [b.kbuf]))
            if layer == 0:
                kbias = smalls["bqk"].t[:, 8 + p:9 + p]
                for th, tw in blocks(T):
                    tasks.append(proj_block(qkk[par], 1024, th, tw, kbias,
                                            [smalls["bqk"]], K(1.0),
                                            [b.kbuf]))

            def cbound():
                kk_ = qkk[par] if layer == 0 else qkk[0]
                ksc = -1.0
                for j in range(2):
                    r0 = 64 * j
                    b.mm_group([lambda t, r0=r0, kk_=kk_: t.matmul(
                        pD.t[:, 0:128], kk_.t[r0:r0 + 64, 0:128],
                        qkq[par].t[r0:r0 + 64, 0:128],
                        start=True, stop=True)],
                        reads=[qkq[par], kk_], writes=[pD])
                    b.op("vector", lambda v, j=j: v.tensor_reduce(
                        tmp[0].t[:, j:j + 1], pD.t[:, 0:128], AX.X,
                        OP.max), reads=[pD], writes=[tmp[0]])
                b.op("tensor", lambda t: t.transpose(
                    pD.t[0:2, 128:256], tmp[0].t[:, 0:2], id32.ap),
                    reads=[tmp[0], id32], writes=[pD])
                b.op("vector", lambda v: v.tensor_reduce(
                    tmp[0].t[0:2, 2:3], pD.t[0:2, 128:256], AX.X, OP.max),
                    reads=[pD], writes=[tmp[0]])
                b.op("vector", lambda v, ksc=ksc: v.tensor_scalar(
                    tmp[0].t[0:2, 3:4], tmp[0].t[0:2, 2:3], ksc, -8.0,
                    OP.mult, OP.add), reads=[tmp[0]], writes=[tmp[0]])
                b.dma("sync",
                      bass.AP(bounce_d.tensor, 10 + 2 * par, [[1, 2],
                                                              [1, 1]]),
                      tmp[0].t[0:2, 3:4], ch="bncw", reads=[tmp[0]],
                      writes=[dbu["bounce"]])
                b.dma("sync", negc.t[:, hA:hA + 2],
                      bass.AP(bounce_d.tensor, 10 + 2 * par,
                              [[0, 128], [1, 2]]),
                      ch="bncr", reads=[dbu["bounce"]],
                      writes=[negc])
            tasks.append(cbound)
            return tasks

        for t_ in pair_tasks(0):
            t_()
        expsc = K(1.0)

        for p in range(NCH):
            par = p % 2
            hA, hB = 2 * p, 2 * p + 1
            kk_ = qkk[par] if layer == 0 else qkk[0]
            tasks = pair_tasks(p + 1) if p + 1 < NCH else []
            for qs in range(NQS):
                qlo = qs * QSP
                qhi = qlo + QSP
                ktmax = (qhi - 1) // 128 if causal else NT - 1

                def rng(kt):
                    qstart = max(qlo, kt * 128) if causal else qlo
                    return qstart, qstart - qlo, qhi - qstart

                def emit_scores(kt):
                    qstart, qoff, width = rng(kt)
                    for j in range(2):
                        pt_, po, pbuf = sS[j][kt % 2]
                        r0 = 64 * j
                        fns = [
                            lambda t, r0=r0, kt=kt, pt_=pt_, po=po,
                            qoff=qoff, width=width, qstart=qstart:
                            t.matmul(
                                pt_[:, po + qoff:po + qoff + width],
                                kk_.t[r0:r0 + 64,
                                      kt * 128:(kt + 1) * 128],
                                qkq[par].t[r0:r0 + 64,
                                           qstart:qstart + width],
                                start=True, stop=not causal)]
                        reads = [qkq[par], kk_]
                        if causal:
                            eb = est[j][par]
                            e0 = qstart - kt * 128
                            fns.append(
                                lambda t, pt_=pt_, po=po, qoff=qoff,
                                width=width, eb=eb, e0=e0: t.matmul(
                                    pt_[:, po + qoff:po + qoff + width],
                                    id16.ap,
                                    eb.t[:, e0:e0 + width],
                                    start=False, stop=True))
                            reads += [eb, id16]
                        b.mm_group(fns, reads=reads, writes=[pbuf])

                def emit_exp(kt):
                    qstart, qoff, width = rng(kt)
                    for j, hh in ((0, hA), (1, hB)):
                        pt_, po, pbuf = sS[j][kt % 2]
                        ptx = ptb[j][kt % 4]
                        b.op("scalar", lambda s, pt_=pt_, po=po, ptx=ptx,
                             hh=hh, qoff=qoff, width=width: s.activation(
                                 ptx.t[:, qoff:qoff + width],
                                 pt_[:, po + qoff:po + qoff + width],
                                 AF.Exp, bias=negc.t[:, hh:hh + 1],
                                 scale=expsc),
                             reads=[pbuf, negc, b.kbuf], writes=[ptx])

                def emit_pv(kt):
                    qstart, qoff, width = rng(kt)
                    for j, hh in ((0, hA), (1, hB)):
                        ct_, co, cbuf = cX[j]
                        ptx = ptb[j][kt % 4]
                        v0 = hh * vstride
                        b.mm_group([
                            lambda t, kt=kt, ct_=ct_, co=co, ptx=ptx,
                            v0=v0, qoff=qoff, width=width: t.matmul(
                                ct_[0:65, co + qoff:co + qoff + width],
                                vtiles[kt].t[:, v0:v0 + 65],
                                ptx.t[:, qoff:qoff + width],
                                start=(kt == 0), stop=(kt == ktmax))],
                            reads=[vtiles[kt], ptx], writes=[cbuf])

                for step in range(ktmax + 4):
                    if step <= ktmax:
                        emit_scores(step)
                    if 1 <= step <= ktmax + 1:
                        emit_exp(step - 1)
                    if 2 <= step <= ktmax + 2:
                        emit_pv(step - 2)
                    if tasks and step <= ktmax:
                        tasks.pop(0)()
                    elif extra and step <= ktmax:
                        extra.pop(0)()

                for j, hh in ((0, hA), (1, hB)):
                    ct_, co, cbuf = cX[j]
                    b.op("vector", lambda v, ct_=ct_, co=co, j=j,
                         cbuf=cbuf, sgj=s64[j]: v.tensor_copy(
                             sgj.t[64:65, 0:QSP],
                             ct_[64:65, co:co + QSP]),
                         reads=[cbuf], writes=[s64[j]])
                    if hh % 2 == 0:
                        b.op("vector", lambda v, ct_=ct_, co=co,
                             qlo=qlo, qhi=qhi, p=p, cbuf=cbuf:
                             v.tensor_copy(
                                 ctx8[p].t[0:64, qlo:qhi],
                                 ct_[0:64, co:co + QSP]),
                             reads=[cbuf], writes=[ctx8[p]])
                    else:
                        b.op("vector", lambda v, ct_=ct_, co=co,
                             cbuf=cbuf: v.tensor_copy(
                                 stg2.t[0:64, 0:QSP],
                                 ct_[0:64, co:co + QSP]),
                             reads=[cbuf], writes=[stg2])
                        b.dma("sync", ctx8[p].t[64:128, qlo:qhi],
                              stg2.t[0:64, 0:QSP], ch="ctxmv",
                              reads=[stg2], writes=[ctx8[p]])
                    b.dma("sync", sig_d[hh:hh + 1, qlo:qhi],
                          s64[j].t[64:65, 0:QSP], ch="sigw",
                          reads=[s64[j]], writes=[dbu["sig0"]])
            for t_ in tasks:
                t_()
        while extra:
            extra.pop(0)()

        HT = H * T
        scols = HT // 128
        b.dma("sync", s64[0].t[:, 0:scols],
              bass.AP(sig_d.tensor, 0, [[scols, 128], [1, scols]]),
              ch="sigr", reads=[dbu["sig0"]], writes=[s64[0]])
        b.op("vector", lambda v: v.reciprocal(
            s64[0].t[:, 0:scols], s64[0].t[:, 0:scols]),
            reads=[s64[0]], writes=[s64[0]])
        b.op("vector", lambda v: v.tensor_copy(
            stg2.t[:, 0:scols], s64[0].t[:, 0:scols]),
            reads=[s64[0]], writes=[stg2])
        b.dma("sync", bass.AP(sig2_d.tensor, 0, [[scols, 128],
                                                 [1, scols]]),
              stg2.t[:, 0:scols], ch="sigw", reads=[stg2],
              writes=[dbu["sig1"]])
        for p in range(NCH):
            for qs in range(NQS):
                qlo = qs * QSP
                for j, hh in ((0, 2 * p), (1, 2 * p + 1)):
                    r0 = 64 * j
                    sgb = stg[j]
                    b.dma("sync", sgb.t[r0:r0 + 64, 0:QSP],
                          sig2_d[0:1, hh * T + qlo:
                                         hh * T + qlo + QSP]
                          .broadcast_to((64, QSP)),
                          ch="sigr", reads=[dbu["sig1"]], writes=[sgb])
                    b.op("vector", lambda v, p=p, r0=r0, qlo=qlo,
                         sgb=sgb: v.tensor_mul(
                             ctx8[p].t[r0:r0 + 64, qlo:qlo + QSP],
                             ctx8[p].t[r0:r0 + 64, qlo:qlo + QSP],
                             sgb.t[r0:r0 + 64, 0:QSP]),
                         reads=[ctx8[p], sgb], writes=[ctx8[p]])
        names = (["qkq0", "qkq1", "qkk0", "qkk1", "wqk0", "wqk1",
                  "stg0", "stg1", "stg2", "s640", "s641"] +
                 [f"pt{h}{i}" for h in range(2) for i in range(4)])
        if causal:
            names += ["est00", "est01", "est10", "est11"]
        b.afree(*names)

    b.phase("wmean")
    wfp2 = [b.aalloc("wf32b", [128, 1024], F32),
            b.aalloc("wf32c", [128, 1024], F32)]
    qwp = [b.aalloc("qw16a", [128, 1024], F16),
           b.aalloc("qw16b", [128, 1024], F16)]
    tqp = [b.aalloc("tq0", [128, 1024], F32),
           b.aalloc("tq1", [128, 1024], F32)]

    wm_tasks = []

    def wmean(dram_name, ncols_total, nin, col_lo):
        nblocks = ncols_total // 128
        nic = nin // 128
        k = 0
        for i in range(nblocks):
            for ic0 in range(0, nic, 8):
                def blk(k=k, i=i, ic0=ic0, dram_name=dram_name, nin=nin):
                    wf = wfp2[k % 2]
                    load_wslices(wf, DI[dram_name], i * 128, 128, nin=nin,
                                 ch="wf" + str(k % 2), cast=False,
                                 ic0=ic0, nic_w=8)
                    b.op("vector", lambda v, k=k, wf=wf: v.tensor_reduce(
                        tmp[1].t[:, k % 1024:k % 1024 + 1],
                        wf.t[:, 0:1024], AX.X, OP.add,
                        apply_absolute_value=True), reads=[wf],
                        writes=[tmp[1]])
                wm_tasks.append(blk)
                k += 1

        def fin(k=k, col_lo=col_lo, nin=nin, ncols_total=ncols_total):
            b.op("vector", lambda v: v.tensor_reduce(
                amax_c.t[:, col_lo:col_lo + 1], tmp[1].t[:, 0:k], AX.X,
                OP.add), reads=[tmp[1]], writes=[amax_c])
            part_scalar_reduce(col_lo, col_lo + 1, OP.add,
                               bcol=2 + col_lo % 8)
            b.op("vector", lambda v: v.tensor_scalar(
                amax_c.t[:, col_lo + 2:col_lo + 3],
                amax_c.t[:, col_lo + 1:col_lo + 2],
                1.0 / (nin * ncols_total), 1e-5, OP.mult, OP.max),
                reads=[amax_c], writes=[amax_c])
            b.op("vector", lambda v: v.reciprocal(
                amax_c.t[:, col_lo + 3:col_lo + 4],
                amax_c.t[:, col_lo + 2:col_lo + 3]),
                reads=[amax_c], writes=[amax_c])
        wm_tasks.append(fin)

    wmean("l1_w1T", HID, E, 16)
    wmean("l1_w2T", E, HID, 20)

    def wquant(wf, dst, n, sw_col, par):
        qw16 = qwp[par]
        tq = tqp[par]
        for o0, ow in blocks(n, 1024):
            b.op("vector", lambda v, o0=o0, ow=ow, wf=wf, qw16=qw16:
                 v.tensor_scalar(
                     qw16.t[:, 0:ow], wf.t[:, o0:o0 + ow],
                     amax_c.t[:, sw_col:sw_col + 1], 1536.0, OP.mult,
                     OP.add), reads=[wf, amax_c], writes=[qw16])
            b.op("vector", lambda v, o0=o0, ow=ow, qw16=qw16, tq=tq:
                 v.tensor_scalar(
                     tq.t[:, 0:ow], qw16.t[:, 0:ow], 1536.0, -1.0,
                     OP.subtract, OP.max), reads=[qw16], writes=[tq])
            b.op("vector", lambda v, o0=o0, ow=ow, tq=tq: v.tensor_scalar(
                dst.t[:, o0:o0 + ow], tq.t[:, 0:ow], 1.0, None, OP.min),
                reads=[tq], writes=[dst])

    b.phase("w1w2quant")
    w1q_spill = b.dram("w1q_spill", [NHC, 128, 1024], BF16).ap()
    dbu["w1q"] = Buf(None, "w1q")
    wqs = [b.aalloc("wqs0", [128, 1024], BF16),
           b.aalloc("wqs1", [128, 1024], BF16)]
    k = 0
    for hc in range(NHC):
        def q1blk(k=k, hc=hc):
            wf = wfp2[k % 2]
            wq_ = wqs[k % 2]
            load_wslices(wf, DI["l1_w1T"], hc * 128, 128,
                         ch="wf" + str(k % 2), cast=False)
            wquant(wf, wq_, 1024, 19, k % 2)
            b.dma("gpsimd", w1q_spill[hc, :, :], wq_.ap,
                  ch="w1qs", reads=[wq_], writes=[dbu["w1q"]])
        wm_tasks.append(q1blk)
        k += 1
    for oc in range(NCH):
        for ic0 in range(0, NHC, 8):
            def q2blk(k=k, oc=oc, ic0=ic0):
                wf = wfp2[k % 2]
                wq_ = wqs[k % 2]
                load_wslices(wf, DI["l1_w2T"], oc * 128, 128, nin=HID,
                             ch="wf" + str(k % 2), cast=False, ic0=ic0,
                             nic_w=8)
                wquant(wf, wq_, 1024, 23, k % 2)
                b.dma("sync",
                      w2q_spill[oc * 128:(oc + 1) * 128,
                                ic0 * 128:(ic0 + 8) * 128],
                      wq_.ap, ch="w2qs", reads=[wq_],
                      writes=[dbu["w2q"]])
            wm_tasks.append(q2blk)
            k += 1

    b.phase("L0attn")
    ctx8_alloc()
    attn_layer(0, vt, 65, extra=wm_tasks)
    b.afree("wf32b", "wf32c", "qw16a", "qw16b", "tq0", "tq1",
            "wqs0", "wqs1")
    b.afree(*[f"vt{i}" for i in range(NT)], "bvx")
    b.tap("ctx0", ctx8)

    def out_proj(w_name, bias_key, resid_dram):
        a8_ = [b.aalloc(f"a{c}", [128, T], F32) for c in range(NCH)]
        wo = [b.aalloc("wo0", [128, 1024], BF16),
              b.aalloc("wo1", [128, 1024], BF16)]
        for oc in range(NCH):
            wb_ = wo[oc % 2]
            load_wslices(wb_, WB[w_name], oc * 128, 128, ch="wo")
            if resid_dram is not None:
                b.dma("sync", a8_[oc].ap,
                      resid_dram[oc * 128:(oc + 1) * 128, :], ch="x",
                      reads=[dbu["xb"]], writes=[a8_[oc]])
            for th, tw in blocks(T):
                ph = pA if (th // 512) % 2 == 0 else pB
                fns = [
                    (lambda t, ic=ic, th=th, tw=tw, ph=ph, wb_=wb_:
                     t.matmul(
                         ph.t[:, 0:tw],
                         wb_.t[:, ic * 128:(ic + 1) * 128],
                         ctx8[ic].t[:, th:th + tw],
                         start=(ic == 0), stop=(ic == NCH - 1)))
                    for ic in range(NCH)]
                b.mm_group(fns, reads=list(ctx8) + [wb_], writes=[ph])
                if resid_dram is not None:
                    b.op("vector", lambda v, oc=oc, th=th, tw=tw, ph=ph:
                         v.scalar_tensor_tensor(
                             a8_[oc].t[:, th:th + tw], ph.t[:, 0:tw],
                             smalls[bias_key].t[:, oc:oc + 1],
                             a8_[oc].t[:, th:th + tw], OP.add, OP.add),
                         reads=[ph, smalls[bias_key], a8_[oc]],
                         writes=[a8_[oc]])
                else:
                    b.op("scalar", lambda s, oc=oc, th=th, tw=tw, ph=ph:
                         s.activation(
                             a8_[oc].t[:, th:th + tw], ph.t[:, 0:tw],
                             AF.Identity,
                             bias=smalls[bias_key].t[:, oc:oc + 1]),
                         reads=[ph, smalls[bias_key]], writes=[a8_[oc]])
        b.afree("wo0", "wo1")
        return a8_

    b.phase("L0outp")
    a8 = out_proj("l0_outT", "ob", xb_spill)
    ctx8_free()
    b.tap("xb1", a8)

    b.phase("ln2mlp")
    for c in range(NCH):
        b.op("vector", lambda v, c=c: v.tensor_copy(c8[c].ap, a8[c].ap),
             reads=[a8[c]], writes=[c8[c]])
    ln_alloc()
    ln_stats_from_c8()
    for c in range(NCH):
        b.op("vector", lambda v, c=c: v.tensor_tensor(
            ft.ap, a8[c].ap, aa.ap, OP.mult), reads=[a8[c], aa],
            writes=[ft])
        b.op("vector", lambda v, c=c: v.tensor_tensor(
            ft.ap, ft.ap, bbt.ap, OP.add), reads=[ft, bbt], writes=[ft])
        b.op("vector", lambda v, c=c: v.tensor_scalar(
            c8[c].ap, ft.ap, smalls["g2"].t[:, c:c + 1],
            smalls["b2"].t[:, c:c + 1], OP.mult, OP.add),
            reads=[ft, smalls["g2"], smalls["b2"]], writes=[c8[c]])
    ln_free()

    hbuf = [b.aalloc(f"h{i}", [128, THL], BF16) for i in range(NHC)]
    wm = [b.aalloc("wm0", [128, 1024], BF16),
          b.aalloc("wm1", [128, 1024], BF16)]
    for th0 in range(0, T, THL):
        for hc in range(NHC):
            wb_ = wm[hc % 2]
            load_wslices(wb_, WB["l0_w1T"], hc * 128, 128, ch="wmlp" + str(hc % 2))
            for q0, qw in blocks(THL):
                ph = pA if (q0 // 512) % 2 == 0 else pB
                fns = [
                    (lambda t, ic=ic, q0=q0, qw=qw, ph=ph, wb_=wb_:
                     t.matmul(
                         ph.t[:, 0:qw],
                         wb_.t[:, ic * 128:(ic + 1) * 128],
                         c8[ic].t[:, th0 + q0:th0 + q0 + qw],
                         start=(ic == 0), stop=(ic == NCH - 1)))
                    for ic in range(NCH)]
                b.mm_group(fns, reads=list(c8) + [wb_], writes=[ph])
                b.op("scalar", lambda s, hc=hc, q0=q0, qw=qw, ph=ph:
                     s.activation(
                         hbuf[hc].t[:, q0:q0 + qw], ph.t[:, 0:qw], AF.Relu,
                         bias=smalls["mb1"].t[:, hc:hc + 1]),
                     reads=[ph, smalls["mb1"]], writes=[hbuf[hc]])
        for oc in range(NCH):
            ph = pC if oc % 2 == 0 else pD
            for hcb in range(0, NHC, 8):
                wb_ = wm[(hcb // 8) % 2]
                t_ = WB["l0_w2T"].tensor
                stride = t_.shape[-1]
                srcp = bass.AP(t_, hcb * 128 * stride + oc * 128,
                               [[stride, 128], [128 * stride, 8], [1, 128]])
                dstp = bass.AP(wb_.t, 0, [[1024, 128], [128, 8], [1, 128]])
                b.dma("sync", dstp, srcp,
                      ch="wmlp" + str((hcb // 8) % 2),
                      reads=[dbu_wbt["l0_w2T"]], writes=[wb_])
                for q0, qw in blocks(THL):
                    fns = [
                        (lambda t, hh2=hh2, q0=q0, qw=qw, ph=ph, wb_=wb_,
                         hcb=hcb: t.matmul(
                             ph.t[:, q0:q0 + qw],
                             wb_.t[:, hh2 * 128:(hh2 + 1) * 128],
                             hbuf[hcb + hh2].t[:, q0:q0 + qw],
                             start=(hcb + hh2 == 0),
                             stop=(hcb + hh2 == NHC - 1)))
                        for hh2 in range(8)]
                    b.mm_group(fns, reads=list(hbuf) + [wb_], writes=[ph])
            for q0, qw in blocks(THL):
                b.op("vector", lambda v, oc=oc, q0=q0, qw=qw, ph=ph,
                     th0=th0: v.scalar_tensor_tensor(
                         a8[oc].t[:, th0 + q0:th0 + q0 + qw],
                         ph.t[:, q0:q0 + qw], smalls["mb2"].t[:, oc:oc + 1],
                         a8[oc].t[:, th0 + q0:th0 + q0 + qw],
                         OP.add, OP.add),
                     reads=[ph, smalls["mb2"], a8[oc]], writes=[a8[oc]])
    b.afree(*[f"h{i}" for i in range(NHC)], "wm0", "wm1")
    b.tap("s_l0", a8)

    b.phase("ffn8")
    wfp = [b.aalloc("wf32", [128, 1024], F32),
           b.aalloc("wf32x", [128, 1024], F32)]
    for oc in range(NCH):
        wf32 = wfp[oc % 2]
        load_wslices(wf32, DI["l0_ffnT"], oc * 128, 128,
                     ch="wf" + str(oc % 2), cast=False)
        b.op("vector", lambda v, oc=oc, wf32=wf32: v.tensor_reduce(
            tmp[0].t[:, oc:oc + 1], wf32.ap, AX.X, OP.max,
            apply_absolute_value=True), reads=[wf32], writes=[tmp[0]])
    b.op("vector", lambda v: v.tensor_reduce(
        amax_c.t[:, 8:9], tmp[0].t[:, 0:NCH], AX.X, OP.max),
        reads=[tmp[0]], writes=[amax_c])
    part_scalar_reduce(8, 9, OP.max, bcol=1)
    glob_prep(9, 10, 11, 12, 224.0, 1e-12)
    b.op("vector", lambda v: v.tensor_scalar(
        amax_c.t[:, 13:14], amax_c.t[:, 4:5], 1.0 / 224.0, None, OP.mult),
        reads=[amax_c], writes=[amax_c])
    b.op("vector", lambda v: v.tensor_scalar(
        amax_c.t[:, 14:15], amax_c.t[:, 10:11], 1.0 / 224.0, None,
        OP.mult), reads=[amax_c], writes=[amax_c])
    b.op("vector", lambda v: v.tensor_tensor(
        amax_c.t[:, 15:16], amax_c.t[:, 13:14], amax_c.t[:, 14:15],
        OP.mult), reads=[amax_c], writes=[amax_c])
    w8a = b.aalloc("w8a", [128, NCH * 1024], FP8)
    for oc in range(NCH):
        wf32 = wfp[oc % 2]
        load_wslices(wf32, DI["l0_ffnT"], oc * 128, 128,
                     ch="wf" + str(oc % 2), cast=False)
        b.op("vector", lambda v, oc=oc, wf32=wf32: v.tensor_scalar(
            w8a.t[:, oc * 1024:(oc + 1) * 1024], wf32.ap,
            amax_c.t[:, 12:13], None, OP.mult),
            reads=[wf32, amax_c], writes=[w8a])
    b.afree("wf32", "wf32x")
    xq8 = [b.aalloc(f"xq8_{c}", [128, T], FP8) for c in range(NCH)]
    for c in range(NCH):
        b.dma("sync", xq8[c].ap, xq8_spill[c * 128:(c + 1) * 128, :],
              ch="q1", reads=[dbu["xq8"]], writes=[xq8[c]])
    gtb = [b.aalloc(f"gt{i}", [128, 512], F32) for i in range(2)]
    xgb = [b.aalloc(f"xg{i}", [128, 512], F32) for i in range(2)]
    for oc in range(NCH):
        for th, tw in blocks(T):
            ph = pA if (th // 512) % 2 == 0 else pB
            gt = gtb[(th // 512) % 2]
            xg = xgb[(th // 512) % 2]
            fns = [
                (lambda t, ic=ic, th=th, tw=tw, ph=ph, oc=oc: t.matmul(
                    ph.t[:, 0:tw],
                    w8a.t[:, oc * 1024 + ic * 128:
                          oc * 1024 + (ic + 1) * 128],
                    xq8[ic].t[:, th:th + tw],
                    start=(ic == 0), stop=(ic == NCH - 1)))
                for ic in range(NCH)]
            b.mm_group(fns, reads=xq8 + [w8a], writes=[ph])
            b.op("scalar", lambda s, oc=oc, tw=tw, ph=ph, gt=gt:
                 s.activation(
                     gt.t[:, 0:tw], ph.t[:, 0:tw], AF.Silu,
                     bias=smalls["fb"].t[:, oc:oc + 1],
                     scale=amax_c.t[:, 15:16]),
                 reads=[ph, smalls["fb"], amax_c], writes=[gt])
            b.dma("scalar", xg.t[:, 0:tw],
                  x_in[oc * 128:(oc + 1) * 128, th:th + tw], ch="xgg",
                  writes=[xg])
            b.op("vector", lambda v, oc=oc, th=th, tw=tw, gt=gt:
                 v.tensor_mul(
                     gt.t[:, 0:tw], gt.t[:, 0:tw],
                     a8[oc].t[:, th:th + tw]),
                 reads=[gt, a8[oc]], writes=[gt])
            b.op("vector", lambda v, tw=tw, gt=gt, xg=xg: v.tensor_tensor(
                xg.t[:, 0:tw], xg.t[:, 0:tw], gt.t[:, 0:tw], OP.add),
                reads=[gt, xg], writes=[xg])
            b.dma("sync", x1_spill[oc * 128:(oc + 1) * 128, th:th + tw],
                  xg.t[:, 0:tw], ch="spill", reads=[xg],
                  writes=[dbu["x1"]])
    b.afree("w8a", "gt0", "gt1", "xg0", "xg1",
            *[f"xq8_{c}" for c in range(NCH)])
    b.afree(*[f"a{c}" for c in range(NCH)])

    b.phase("L1ln")
    xn1 = [b.aalloc(f"n{c}", [128, T], F32) for c in range(NCH)]
    for c in range(NCH):
        b.dma("sync", xn1[c].ap, x1_spill[c * 128:(c + 1) * 128, :],
              ch="x", reads=[dbu["x1"]], writes=[xn1[c]])
        b.op("vector", lambda v, c=c: v.tensor_copy(c8[c].ap, xn1[c].ap),
             reads=[xn1[c]], writes=[c8[c]])
    ln_alloc()
    ln_stats_from_c8()
    for c in range(NCH):
        b.op("vector", lambda v, c=c: v.tensor_tensor(
            xn1[c].ap, xn1[c].ap, aa.ap, OP.mult),
            reads=[xn1[c], aa], writes=[xn1[c]])
        b.op("vector", lambda v, c=c: v.tensor_tensor(
            xn1[c].ap, xn1[c].ap, bbt.ap, OP.add),
            reads=[xn1[c], bbt], writes=[xn1[c]])
        b.op("vector", lambda v, c=c: v.tensor_copy(c8[c].ap, xn1[c].ap),
             reads=[xn1[c]], writes=[c8[c]])
    ln_free()
    b.tap("xn1", xn1)

    b.phase("q1")
    rowm = b.aalloc("rowm", [128, T], F32)
    scl = b.aalloc("scl", [128, T], F32)
    rs1 = b.aalloc("rs1", [128, T], F32)
    qf16 = b.aalloc("qf16", [128, T], F16)
    b.op("scalar", lambda s: s.activation(
        rowm.ap, xn1[0].ap, AF.Square), reads=[xn1[0]], writes=[rowm])
    for c in range(1, NCH):
        b.op("scalar", lambda s, c=c: s.activation(
            ft.ap, xn1[c].ap, AF.Square), reads=[xn1[c]], writes=[ft])
        b.op("vector", lambda v: v.tensor_max(rowm.ap, rowm.ap, ft.ap),
             reads=[rowm, ft], writes=[rowm])
    featmax_scales(rowm, scl, rs1)
    for c in range(NCH):
        b.op("vector", lambda v, c=c: v.tensor_mul(
            ft.ap, xn1[c].ap, scl.ap), reads=[xn1[c], scl], writes=[ft])
        b.op("vector", lambda v: v.tensor_scalar(
            qf16.ap, ft.ap, 1536.0, None, OP.add), reads=[ft],
            writes=[qf16])
        b.op("vector", lambda v: v.tensor_scalar(
            ft.ap, qf16.ap, 1536.0, None, OP.subtract),
            reads=[qf16], writes=[ft])
        b.dma("gpsimd", xq1_spill[c * 128:(c + 1) * 128, :], ft.ap,
              ch="q1s", reads=[ft], writes=[dbu["xq1"]])
    b.afree("rowm", "qf16", *[f"n{c}" for c in range(NCH)])

    b.phase("V1")
    vt1 = [b.aalloc(f"u{i}", [128, 66], BF16) for i in range(NT)]
    bv1x = b.aalloc("bv1x", [128, 66], F32)
    b.dma("sync", bv1x.t[:, 0:65],
          DI["l1_bv_ext"][0:1, :].broadcast_to((128, HD + 1)),
          ch="const", writes=[bv1x])
    wv1 = b.aalloc("wv1", [128, NCH * 65], BF16)
    load_wslices(wv1, WB["l1_vT_ext"], 0, HD + 1, ch="wv")
    for tt in range(NT):
        fns = [
            (lambda t, ic=ic, tt=tt: t.matmul(
                pA.t[:, 0:65],
                c8[ic].t[:, tt * 128:(tt + 1) * 128],
                wv1.t[:, ic * 65:(ic + 1) * 65],
                start=(ic == 0), stop=(ic == NCH - 1)))
            for ic in range(NCH)]
        b.mm_group(fns, reads=list(c8) + [wv1], writes=[pA])
        b.op("vector", lambda v, tt=tt: v.tensor_tensor(
            vt1[tt].t[:, 0:65], pA.t[:, 0:65], bv1x.t[:, 0:65], OP.add),
            reads=[pA, bv1x], writes=[vt1[tt]])
    b.afree("wv1")

    b.phase("L1attn")
    ctx8_alloc()
    attn_layer(1, vt1, 0)
    b.afree(*[f"u{i}" for i in range(NT)], "bv1x")
    b.tap("ctx1", ctx8)

    b.phase("L1outp")
    a8 = out_proj("l1_WoT", "bo1", None)
    b.tap("s_l1", a8)

    for c in range(NCH):
        b.dma("sync", c8[c].ap, xq1_spill[c * 128:(c + 1) * 128, :],
              ch="q1", reads=[dbu["xq1"]], writes=[c8[c]])

    b.op("vector", lambda v: v.tensor_scalar(
        rs1.ap, rs1.ap, amax_c.t[:, 18:19], None, OP.mult),
        reads=[rs1, amax_c], writes=[rs1])
    b.phase("passA")
    rowm2 = b.aalloc("rowm2", [128, T], F32)
    b.op("vector", lambda v: v.memset(rowm2.ap, 0.0), writes=[rowm2])
    h1g = [b.aalloc("h1g0", [128, T + 2], BF16)] * 2
    h2f = [b.aalloc(f"h2f{i}", [128, 512], F32) for i in range(2)]
    sqf = [b.aalloc("sqf0", [128, 512], F32)] * 2
    w1qb = [b.aalloc("w1qb0", [128, 1024], BF16),
            b.aalloc("w1qb1", [128, 1024], BF16)]
    cdg = [b.aalloc("cdg0", [128, 3 * 128], BF16),
           b.aalloc("cdg1", [128, 3 * 128], BF16)]
    for hc in range(NHC):
        par = hc % 2
        wq_ = w1qb[par]
        hg = h1g[par]
        cd = cdg[par]
        b.dma("sync", wq_.ap, w1q_spill[hc, :, :], ch="w1ql" + str(par),
              reads=[dbu["w1q"]], writes=[wq_])
        b.dma("sync", cd.t[:, 0:3 * 128],
              WB["l1_convdiag"][hc, :, :], ch="cdg" + str(par),
              reads=[dbu_wbt["l1_convdiag"]], writes=[cd])
        for th, tw in blocks(T):
            ph = pA if (th // 512 + hc) % 2 == 0 else pB
            fns = [
                (lambda t, ic=ic, th=th, tw=tw, ph=ph, wq_=wq_: t.matmul(
                    ph.t[:, 0:tw],
                    wq_.t[:, ic * 128:(ic + 1) * 128],
                    c8[ic].t[:, th:th + tw],
                    start=(ic == 0), stop=(ic == NCH - 1)))
                for ic in range(NCH)]
            b.mm_group(fns, reads=list(c8) + [wq_], writes=[ph])
            b.op("vector", lambda v, th=th, tw=tw, ph=ph, hg=hg:
                 v.tensor_mul(
                     hg.t[:, 1 + th:1 + th + tw], ph.t[:, 0:tw],
                     rs1.t[:, th:th + tw]), reads=[ph, rs1], writes=[hg])
        b.op("vector", lambda v, hg=hg: v.memset(hg.t[:, 0:1], 0.0),
             writes=[hg])
        b.op("vector", lambda v, hg=hg: v.memset(hg.t[:, T + 1:T + 2],
                                                 0.0), writes=[hg])
        for th, tw in blocks(T):
            ph = pC if (th // 512 + hc) % 2 == 0 else pD
            h2 = h2f[(th // 512) % 2]
            sq = sqf[(th // 512) % 2]
            fns = [
                (lambda t, j=j, th=th, tw=tw, ph=ph, hg=hg, cd=cd:
                 t.matmul(
                     ph.t[:, 0:tw],
                     cd.t[:, j * 128:(j + 1) * 128],
                     hg.t[:, th + j:th + j + tw],
                     start=(j == 0), stop=(j == 2)))
                for j in range(3)]
            b.mm_group(fns, reads=[cd, hg], writes=[ph])
            b.op("scalar", lambda s, hc=hc, tw=tw, ph=ph, h2=h2:
                 s.activation(
                     h2.t[:, 0:tw], ph.t[:, 0:tw], AF.Silu,
                     bias=smalls["cb"].t[:, hc:hc + 1]),
                 reads=[ph, smalls["cb"]], writes=[h2])
            b.op("scalar", lambda s, tw=tw, h2=h2, sq=sq: s.activation(
                sq.t[:, 0:tw], h2.t[:, 0:tw], AF.Square),
                reads=[h2], writes=[sq])
            b.op("vector", lambda v, th=th, tw=tw, sq=sq: v.tensor_max(
                rowm2.t[:, th:th + tw], rowm2.t[:, th:th + tw],
                sq.t[:, 0:tw]), reads=[rowm2, sq], writes=[rowm2])
            b.dma("scalar" if (th // 512) % 2 == 0 else "gpsimd",
                  h2_spill[hc * 128:(hc + 1) * 128, th:th + tw],
                  h2.t[:, 0:tw], ch="h2s" + str((th // 512) % 2),
                  reads=[h2], writes=[dbu["h2"]])
    b.afree("h1g0", "h2f0", "h2f1", "sqf0",
            "w1qb0", "w1qb1", "cdg0", "cdg1")

    scl2 = scl
    rs2 = rs1
    featmax_scales(rowm2, scl2, rs2)
    b.afree("rowm2")

    b.phase("passB")
    qf16b = [b.aalloc(f"qf{i}", [128, 512], F16) for i in range(2)]
    ftb = [b.aalloc(f"ftb{i}", [128, 512], F32) for i in range(2)]
    xg2 = [b.aalloc(f"xg2{i}", [128, 512], F32) for i in range(2)]
    gt2 = [b.aalloc(f"gt2{i}", [128, 512], F32) for i in range(2)]
    h2c = [b.aalloc(f"h2c{i}", [128, 512], F32) for i in range(2)]
    w2s = [b.aalloc("w2s0", [128, NHC * 128], BF16),
           b.aalloc("w2s1", [128, NHC * 128], BF16)]
    for tc in range(NTC):
        th = tc * TCH
        tw = TCH
        for hc in range(NHC):
            par = hc % 2
            dstb = ctx8[hc % NCH]
            doff = (hc // NCH) * TCH
            b.dma("sync", h2c[par].t[:, 0:tw],
                  h2_spill[hc * 128:(hc + 1) * 128, th:th + tw],
                  ch="h2l" + str(par), reads=[dbu["h2"]],
                  writes=[h2c[par]])
            b.op("vector", lambda v, th=th, tw=tw, par=par: v.tensor_mul(
                ftb[par].t[:, 0:tw], h2c[par].t[:, 0:tw],
                scl2.t[:, th:th + tw]),
                reads=[h2c[par], scl2], writes=[ftb[par]])
            b.op("vector", lambda v, tw=tw, par=par: v.tensor_scalar(
                qf16b[par].t[:, 0:tw], ftb[par].t[:, 0:tw], 1536.0, None,
                OP.add), reads=[ftb[par]], writes=[qf16b[par]])
            b.op("vector", lambda v, dstb=dstb, doff=doff, tw=tw, par=par:
                 v.tensor_scalar(
                     dstb.t[:, doff:doff + tw], qf16b[par].t[:, 0:tw],
                     1536.0, None, OP.subtract),
                 reads=[qf16b[par]], writes=[dstb])
        for oc in range(NCH):
            par = oc % 2
            wsb = w2s[par]
            b.dma("gpsimd", wsb.ap,
                  w2q_spill[oc * 128:(oc + 1) * 128, :],
                  ch="w2l" + str(par), reads=[dbu["w2q"]], writes=[wsb])
            ph = pA if par == 0 else pB
            fns = [
                (lambda t, hc=hc, ph=ph, wsb=wsb, tw=tw: t.matmul(
                    ph.t[:, 0:tw],
                    wsb.t[:, hc * 128:(hc + 1) * 128],
                    ctx8[hc % NCH].t[:, (hc // NCH) * TCH:
                                     (hc // NCH) * TCH + tw],
                    start=(hc == 0), stop=(hc == NHC - 1)))
                for hc in range(NHC)]
            b.mm_group(fns, reads=list(ctx8) + [wsb], writes=[ph])
            b.op("vector", lambda v, th=th, tw=tw, ph=ph, par=par:
                 v.tensor_mul(
                     gt2[par].t[:, 0:tw], ph.t[:, 0:tw],
                     rs2.t[:, th:th + tw]),
                 reads=[ph, rs2], writes=[gt2[par]])
            b.op("scalar", lambda s, tw=tw, par=par: s.activation(
                gt2[par].t[:, 0:tw], gt2[par].t[:, 0:tw], AF.Silu,
                scale=amax_c.t[:, 22:23]),
                reads=[gt2[par], amax_c], writes=[gt2[par]])
            b.dma("sync", xg2[par].t[:, 0:tw],
                  x1_spill[oc * 128:(oc + 1) * 128, th:th + tw],
                  ch="xg2" + str(par), reads=[dbu["x1"]],
                  writes=[xg2[par]])
            b.op("vector", lambda v, oc=oc, tw=tw, par=par, th=th:
                 v.tensor_mul(
                     gt2[par].t[:, 0:tw], gt2[par].t[:, 0:tw],
                     a8[oc].t[:, th:th + tw]), reads=[gt2[par], a8[oc]],
                 writes=[gt2[par]])
            b.op("vector", lambda v, tw=tw, par=par: v.tensor_tensor(
                xg2[par].t[:, 0:tw], xg2[par].t[:, 0:tw],
                gt2[par].t[:, 0:tw], OP.add),
                reads=[gt2[par], xg2[par]], writes=[xg2[par]])
            b.dma("scalar", out_d[oc * 128:(oc + 1) * 128, th:th + tw],
                  xg2[par].t[:, 0:tw], ch="out" + str(par),
                  reads=[xg2[par]])

    for base in ("D:out0.", "D:out1."):
        for i in range(Bld._CH_DEPTH):
            och = base + str(i)
            if och in b.sems:
                b.eng["sync"].wait_ge(b.sems[och], b.cnt[och])
    return b

def _stage_weights(inp):
    w = {}
    tr = lambda a: np.ascontiguousarray(np.asarray(a, np.float32).T)
    col = lambda v: np.ascontiguousarray(
        np.asarray(v, np.float32).reshape(-1, 128).T)
    rel = np.asarray(inp["L0_attn_rel_pos"], np.float32)
    Tl = _T
    ki = np.arange(128)[:, None]
    fi = np.arange(Tl)[None, :]
    d = fi - ki
    strip = np.where((d >= 0) & (d < 2048),
                     rel[:, np.clip(d, 0, 2047)], np.float32(-1e9))
    w["l0_relstrip"] = np.ascontiguousarray(strip, np.float32)
    ipw = np.asarray(inp["L0_attn_in_proj_w"], np.float32)
    ipb = np.asarray(inp["L0_attn_in_proj_b"], np.float32)
    w["l0_qkT"] = np.ascontiguousarray(ipw[:2 * E].T)
    w["l0_bqk_c"] = col(ipb[:2 * E])
    wv = ipw[2 * E:]
    bv = ipb[2 * E:]
    vext = np.zeros((E, H * (HD + 1)), np.float32)
    bvext = np.zeros((1, H * (HD + 1)), np.float32)
    for h in range(H):
        vext[:, h * 65:h * 65 + 64] = wv[h * 64:(h + 1) * 64].T
        bvext[0, h * 65:h * 65 + 64] = bv[h * 64:(h + 1) * 64]
        bvext[0, h * 65 + 64] = 1.0
    w["l0_vT_ext"] = vext
    w["l0_bv_ext"] = bvext
    w["l0_outT"] = tr(inp["L0_attn_out_w"])
    w["l0_out_b_c"] = col(inp["L0_attn_out_b"])
    w["l0_w1T"] = tr(inp["L0_attn_w1"])
    w["l0_b1_c"] = col(inp["L0_attn_b1"])
    w["l0_w2T"] = tr(inp["L0_attn_w2"])
    w["l0_b2_c"] = col(inp["L0_attn_b2"])
    w["l0_ln1_g_c"] = col(inp["L0_attn_ln1_g"])
    w["l0_ln1_b_c"] = col(inp["L0_attn_ln1_b"])
    w["l0_ln2_g_c"] = col(inp["L0_attn_ln2_g"])
    w["l0_ln2_b_c"] = col(inp["L0_attn_ln2_b"])
    w["l0_ffnT"] = tr(inp["L0_ffn_weight"])
    w["l0_ffn_b_c"] = col(inp["L0_ffn_bias"])
    w["l1_WqT"] = tr(inp["L1_attn_Wq"])
    w["l1_bq_c"] = col(inp["L1_attn_bq"])
    w["l1_WkT"] = tr(inp["L1_attn_Wk"])
    wv1 = np.asarray(inp["L1_attn_Wv"], np.float32)
    bv1 = np.asarray(inp["L1_attn_bv"], np.float32)
    vext1 = np.zeros((E, HD + 1), np.float32)
    vext1[:, 0:HD] = wv1.T
    bvext1 = np.zeros((1, HD + 1), np.float32)
    bvext1[0, 0:HD] = bv1
    bvext1[0, HD] = 1.0
    w["l1_vT_ext"] = vext1
    w["l1_bv_ext"] = bvext1
    w["l1_WoT"] = tr(inp["L1_attn_Wo"])
    w["l1_bo_c"] = col(inp["L1_attn_bo"])
    w["l1_w1T"] = tr(inp["L1_ffn_w1"])
    w["l1_conv_b_c"] = col(inp["L1_ffn_conv_b"])
    w["l1_w2T"] = tr(inp["L1_ffn_w2"])
    cw = np.asarray(inp["L1_ffn_conv_w"], np.float32)[:, 0, :]
    diag = np.zeros((NHC, 3, 128, 128), np.float32)
    idx = np.arange(128)
    for hc in range(NHC):
        for j in range(3):
            diag[hc, j, idx, idx] = cw[hc * 128:(hc + 1) * 128, j]
    w["l1_convdiag"] = np.ascontiguousarray(
        diag.transpose(0, 2, 1, 3).reshape(NHC, 128, 3 * 128))
    return w


def kernel(**inputs):
    T = _T
    x = np.asarray(inputs["L0_attn_x"])
    B = x.shape[0]
    w = _stage_weights(inputs)
    bld = build_program(T=T, n_cores=B)
    bld.nc.finalize()
    in_maps = []
    for i in range(B):
        m = dict(w)
        m["xT"] = np.ascontiguousarray(x[i, :T, :].astype(np.float32).T)
        in_maps.append(m)
    res = run_bass_kernel_spmd(bld.nc, in_maps, core_ids=list(range(B)))
    outs = [np.ascontiguousarray(res.results[i]["outT"].T) for i in range(B)]
    return np.stack(outs).astype(np.float32)


# version 31
# speedup vs baseline: 1.0274x; 1.0274x over previous
import contextlib

import numpy as np

import concourse.bass as bass
import concourse.mybir as mybir
from concourse.bass_utils import run_bass_kernel_spmd

dt = mybir.dt
F32, BF16, F16, FP8 = dt.float32, dt.bfloat16, dt.float16, dt.float8e4
AF = mybir.ActivationFunctionType
OP = mybir.AluOpType
AX = mybir.AxisListType

NCORES = 8
E = 1024
H = 16
HD = 64
HID = 4096
NCH = E // 128
NHC = HID // 128
EPS = 1e-5
FP8_MAX = 448.0
ZPAD = 128

_T = 2048
_DEBUG_TAPS = ()



class Buf:
    __slots__ = ("t", "w", "readers", "name")

    def __init__(self, tile, name=""):
        self.t = tile
        self.w = None
        self.readers = []
        self.name = name

    def __getitem__(self, key):
        return self.t[key]

    @property
    def ap(self):
        return self.t[:]


class Bld:
    def __init__(self, T):
        self.T = T
        self.nc = bass.Bass("TRN2", target_bir_lowering=False)
        self.es = contextlib.ExitStack()
        nc = self.nc
        self.eng = {"tensor": nc.tensor, "vector": nc.vector,
                    "scalar": nc.scalar, "gpsimd": nc.gpsimd, "sync": nc.sync}
        self.sems = {}
        self.cnt = {}
        for name in ("mT", "mV", "mS", "mG"):
            self.sems[name] = self.es.enter_context(nc.semaphore(name))
            self.cnt[name] = 0
        self.marksem = {"tensor": "mT", "vector": "mV", "scalar": "mS",
                        "gpsimd": "mG"}
        self.waited = {e: {} for e in self.eng}
        self._konst_tile = None
        self._konst_map = {}
        self.phases = []
        self._seg_free = []
        self._seg_used = {}
        self._seg_tomb = []
        self._arena_lo = None
        self._arena_hi = None

    def phase(self, name):
        self.phases.append((name, self.nc.n_instructions()))

    def konst(self, value):
        if self._konst_tile is None:
            self._konst_tile = Buf(self.nc.alloc_sbuf_tensor(
                "konst", [128, 32], F32), "konst")
        if value not in self._konst_map:
            i = len(self._konst_map)
            assert i < 32
            self._konst_map[value] = i
            self.op("vector",
                    lambda v, i=i: v.memset(
                        self._konst_tile.t[:, i:i + 1], float(value)),
                    writes=[self._konst_tile])
        return self._konst_tile.t[:, self._konst_map[value]:
                                  self._konst_map[value] + 1]

    @property
    def kbuf(self):
        return self._konst_tile

    def sb(self, name, shape, dty):
        return Buf(self.nc.alloc_sbuf_tensor(name, shape, dty), name)

    def ps(self, name, shape, dty=F32):
        return Buf(self.nc.alloc_psum_tensor(name, shape, dty), name)

    def dram(self, name, shape, dty, kind="Internal"):
        return self.nc.dram_tensor(name, shape, dty, kind=kind)

    def _arena_init(self):
        if self._arena_lo is not None:
            return
        self.konst(1.0)
        lo = self.nc.sbuf_base
        hi = self.nc.sbuf_top
        self._arena_lo, self._arena_hi = lo, hi
        self._seg_free = [[lo, hi]]
        self.nc.sbuf_base = hi

    def aalloc(self, name, shape, dty):
        self._arena_init()
        nbytes = int(np.prod(shape[1:])) * {F32: 4, BF16: 2, F16: 2,
                                            FP8: 1}[dty]
        nbytes = (nbytes + 31) // 32 * 32
        for seg in self._seg_free:
            if seg[1] - seg[0] >= nbytes:
                off = seg[0]
                seg[0] += nbytes
                tile = self.nc.alloc_sbuf_tensor_at(
                    f"{name}@{off}", shape, dty, offset=off)
                buf = Buf(tile, name)
                for (s, e, ticks) in self._seg_tomb:
                    if s < off + nbytes and e > off:
                        buf.readers.extend(ticks)
                self._seg_used[name] = (off, off + nbytes, buf)
                return buf
        raise MemoryError(
            f"arena: no room for {name} ({nbytes}B/partition); "
            f"free={[(s, e - s) for s, e in self._seg_free]}")

    def afree(self, *names):
        for name in names:
            off, end, buf = self._seg_used.pop(name)
            ticks = list(buf.readers)
            if buf.w is not None:
                ticks.append(buf.w)
            self._seg_tomb = [(s, e, tk) for (s, e, tk) in self._seg_tomb
                              if not (s >= off and e <= end)]
            self._seg_tomb.append((off, end, ticks))
            self._seg_free.append([off, end])
        self._seg_free.sort()
        out = []
        for seg in self._seg_free:
            if out and out[-1][1] == seg[0]:
                out[-1][1] = seg[1]
            else:
                out.append(seg)
        self._seg_free = out

    def _mark(self, instr, sem, inc):
        instr.then_inc(self.sems[sem], inc)
        self.cnt[sem] += inc
        return (sem, self.cnt[sem])

    def chan(self, ch):
        name = "D:" + ch
        if name not in self.sems:
            self.sems[name] = self.es.enter_context(self.nc.semaphore(name))
            self.cnt[name] = 0
        return name

    _CH_DEPTH = 2

    def _subchan(self, ch):
        key = "R:" + ch
        i = self.cnt.get(key, 0)
        self.cnt[key] = (i + 1) % self._CH_DEPTH
        return self.chan(f"{ch}.{i}")

    def need(self, eng, ticket):
        if ticket is None:
            return
        sem, val = ticket
        if sem.startswith("D:"):
            val = self.cnt[sem]
        w = self.waited[eng]
        if w.get(sem, 0) < val:
            self.eng[eng].wait_ge(self.sems[sem], val)
            w[sem] = val

    def _pre(self, eng, reads, writes):
        for b in reads:
            if b is not None and b.w is not None:
                self.need(eng, b.w)
        for b in writes:
            if b is None:
                continue
            if b.w is not None:
                self.need(eng, b.w)
            for t in b.readers:
                self.need(eng, t)

    def _post(self, tick, reads, writes):
        for b in reads:
            if b is not None:
                b.readers.append(tick)
        for b in writes:
            if b is not None:
                b.w = tick
                b.readers = []
        return tick

    def op(self, eng, fn, reads=(), writes=()):
        self._pre(eng, reads, writes)
        instr = fn(self.eng[eng])
        tick = self._mark(instr, self.marksem[eng], 1)
        return self._post(tick, reads, writes)

    def dma(self, eng, out_ap, in_ap, ch="ld", reads=(), writes=()):
        self._pre(eng, reads, writes)
        chn = self._subchan(ch)
        self.need(eng, (chn, self.cnt[chn]))
        instr = self.eng[eng].dma_start(out=out_ap, in_=in_ap)
        tick = self._mark(instr, chn, 16)
        return self._post(tick, reads, writes)

    def mm_group(self, fns, reads=(), writes=()):
        self._pre("tensor", reads, writes)
        instr = None
        for fn in fns:
            instr = fn(self.nc.tensor)
        tick = self._mark(instr, "mT", 1)
        return self._post(tick, reads, writes)

    def rebind(self, buf, name=""):
        nb = Buf(buf.t, name or buf.name)
        nb.w = buf.w
        nb.readers = list(buf.readers)
        return nb

    def tap(self, name, bufs, rows=128, cols=None):
        if name not in _DEBUG_TAPS:
            return
        n = len(bufs)
        w = cols or bufs[0].t.shape[-1]
        d = self.dram(f"dbg_{name}", [n, rows, w], bufs[0].t.dtype,
                      kind="ExternalOutput")
        for i, bb in enumerate(bufs):
            self.dma("scalar", d[i, :rows, :], bb.t[:rows, :w], ch="dbg",
                     reads=[bb])



def build_program(T=None, n_cores=NCORES):
    T = T or _T
    NT = T // 128
    QSP = min(512, T)
    NQS = T // QSP
    TCH = min(512, T // 4)
    NTC = T // TCH
    THL = min(1024, T)
    PW = H * (HD + 1)
    ELEN = ZPAD + T

    b = Bld(T)
    nc = b.nc
    K = b.konst

    DI = {}

    def din(name, shape, dty=F32):
        DI[name] = b.dram(name, shape, dty, kind="ExternalInput").ap()
        return DI[name]

    x_in = din("xT", [E, T])
    din("l0_relstrip", [H, 128, T])
    din("l0_qkT", [E, 2 * E])
    din("l0_bqk_c", [128, 16])
    din("l0_vT_ext", [E, PW])
    din("l0_bv_ext", [1, PW])
    din("l0_outT", [E, E])
    din("l0_out_b_c", [128, NCH])
    din("l0_w1T", [E, HID])
    din("l0_b1_c", [128, NHC])
    din("l0_w2T", [HID, E])
    din("l0_b2_c", [128, NCH])
    din("l0_ln1_g_c", [128, NCH])
    din("l0_ln1_b_c", [128, NCH])
    din("l0_ln2_g_c", [128, NCH])
    din("l0_ln2_b_c", [128, NCH])
    din("l0_ffnT", [E, E])
    din("l0_ffn_b_c", [128, NCH])
    din("l1_WqT", [E, E])
    din("l1_bq_c", [128, NCH])
    din("l1_WkT", [E, HD])
    din("l1_vT_ext", [E, HD + 1])
    din("l1_bv_ext", [1, HD + 1])
    din("l1_WoT", [E, E])
    din("l1_bo_c", [128, NCH])
    din("l1_w1T", [E, HID])
    din("l1_convdiag", [NHC, 128, 3 * 128])
    din("l1_conv_b_c", [128, NHC])
    din("l1_w2T", [HID, E])

    out_d = b.dram("outT", [E, T], F32, kind="ExternalOutput").ap()

    xb_spill = b.dram("xb_spill", [E, T], F32).ap()
    x1_spill = b.dram("x1_spill", [E, T], F32).ap()
    xq8_spill = b.dram("xq8_spill", [E, T], FP8).ap()
    xq1_spill = b.dram("xq1_spill", [E, T], BF16).ap()
    h2_spill = b.dram("h2_spill", [HID, T], F32).ap()
    w2q_spill = b.dram("w2q_spill", [NCH * 128, NHC * 128], BF16).ap()
    cc_in_d = b.dram("cc_in", [1, 8], F32)
    cc_out_d = b.dram("cc_out", [1, 8], F32)
    WB = {}
    for nm, shp in (("l0_qkT", [E, 2 * E]), ("l0_vT_ext", [E, PW]),
                    ("l0_outT", [E, E]), ("l0_w1T", [E, HID]),
                    ("l0_w2T", [HID, E]), ("l1_WqT", [E, E]),
                    ("l1_WkT", [E, HD]), ("l1_vT_ext", [E, HD + 1]),
                    ("l1_WoT", [E, E]),
                    ("l1_convdiag", [NHC, 128, 3 * 128]),
                    ("l0_relstrip", [H, 128, T])):
        WB[nm] = b.dram(nm + "_bf", shp, BF16).ap()
    bounce_d = b.dram("bounce", [1, 64], F32).ap()
    stok_d = b.dram("stok_d", [2, T], F32).ap()
    sig_d = b.dram("sig_d", [H, T], F32).ap()
    sig2_d = b.dram("sig2_d", [1, H * T], BF16).ap()
    dbu = {nm: Buf(None, nm) for nm in
           ("xb", "x1", "xq8", "xq1", "h2", "w2q", "bounce", "stok",
            "sig0", "sig1")}

    c8 = [b.sb(f"c{c}", [128, T], BF16) for c in range(NCH)]
    ctx8 = [None] * NCH

    def ctx8_alloc():
        for c in range(NCH):
            ctx8[c] = b.aalloc(f"ctx{c}", [128, T], BF16)

    def ctx8_free():
        b.afree(*[f"ctx{c}" for c in range(NCH)])
    ones_bf = b.sb("ones_bf", [128, 128], BF16)
    id32 = b.sb("id32", [128, 128], F32)
    id16 = b.sb("id16", [128, 128], BF16)
    ft = b.sb("ft", [128, T], F32)
    aa = bbt = None

    def ln_alloc():
        nonlocal aa, bbt
        aa = b.aalloc("aa", [128, T], F32)
        bbt = b.aalloc("bbt", [128, T], F32)

    def ln_free():
        b.afree("aa", "bbt")
    tmp = [b.sb(f"tmp{i}", [128, 1024], F32) for i in range(2)]
    wb = [b.sb(f"wb{i}", [128, 1024], BF16) for i in range(2)]
    smalls = {}
    for nm, w_ in (("g1", NCH), ("b1", NCH), ("g2", NCH), ("b2", NCH),
                   ("ob", NCH), ("fb", NCH), ("bo1", NCH), ("bq1", NCH),
                   ("bq1s", NCH), ("mb1", NHC), ("mb2", NCH), ("cb", NHC),
                   ("bqk", 16), ("bqks", 8), ("negc", 16), ("sc", 32)):
        smalls[nm] = b.sb(nm, [128, w_], F32)
    amax_c = smalls["sc"]

    pA = b.ps("pA", [128, 1024])
    pB = b.ps("pB", [128, 1024])
    pC = b.ps("pC", [128, 1024])
    pD = b.ps("pD", [128, 1024])

    def blocks(total, step=512):
        return [(o, min(step, total - o)) for o in range(0, total, step)]

    b.op("vector", lambda v: v.memset(ones_bf.ap, 1.0), writes=[ones_bf])
    b.op("vector", lambda v: v.memset(id32.ap, 1.0), writes=[id32])
    b.op("gpsimd", lambda g: g.affine_select(
        id32.ap, id32.ap, [[1, 128]], OP.is_equal, 0.0, base=0,
        channel_multiplier=-1), reads=[id32], writes=[id32])
    b.op("vector", lambda v: v.tensor_copy(id16.ap, id32.ap),
         reads=[id32], writes=[id16])
    def emit_smalls():
        for nm, key in (("l0_ln1_g_c", "g1"), ("l0_ln1_b_c", "b1"),
                        ("l0_ln2_g_c", "g2"), ("l0_ln2_b_c", "b2"),
                        ("l0_out_b_c", "ob"), ("l0_ffn_b_c", "fb"),
                        ("l1_bo_c", "bo1"), ("l1_bq_c", "bq1"),
                        ("l0_b1_c", "mb1"), ("l0_b2_c", "mb2"),
                        ("l1_conv_b_c", "cb"), ("l0_bqk_c", "bqk")):
            b.dma("sync", smalls[key].ap, DI[nm][:, :], ch="const",
                  writes=[smalls[key]])
        b.op("vector", lambda v: v.tensor_scalar(
            smalls["bq1s"].ap, smalls["bq1"].ap, 0.125, None, OP.mult),
            reads=[smalls["bq1"]], writes=[smalls["bq1s"]])
        b.op("vector", lambda v: v.tensor_scalar(
            smalls["bqks"].ap, smalls["bqk"].t[:, 0:8], 0.125, None,
            OP.mult), reads=[smalls["bqk"]], writes=[smalls["bqks"]])

    dbu_wbt = {}

    def emit_precasts():
        for nm in ("l0_vT_ext", "l0_qkT", "l0_relstrip", "l0_outT",
                   "l0_w1T", "l0_w2T", "l1_WqT", "l1_WkT", "l1_vT_ext",
                   "l1_WoT", "l1_convdiag"):
            t_ = DI[nm].tensor
            tot = 1
            for d_ in t_.shape:
                tot *= int(d_)
            dbu_wbt[nm] = Buf(None, nm + "cast")
            step = 2 ** 20
            for o0 in range(0, tot, step):
                ow = min(step, tot - o0)
                b.dma("gpsimd",
                      bass.AP(WB[nm].tensor, o0, [[1, 1], [1, ow]]),
                      bass.AP(t_, o0, [[1, 1], [1, ow]]),
                      ch="wc" + str((o0 // step) % 2),
                      writes=[dbu_wbt[nm]])

    def ln_stats_from_c8():
        sqb = [wb[0], wb[1]]
        pos = [(pA, 0), (pA, 512), (pB, 0), (pB, 512)]
        pos2 = [(pC, 0), (pC, 512), (pD, 0), (pD, 512)]
        for c in range(NCH):
            for tc in range(NTC):
                sl = slice(tc * TCH, (tc + 1) * TCH)
                ph, po = pos[tc % 4]
                b.mm_group([lambda t, c=c, sl=sl, ph=ph, po=po: t.matmul(
                    ph.t[:, po:po + TCH], ones_bf.ap, c8[c].t[:, sl],
                    start=(c == 0), stop=(c == NCH - 1))],
                    reads=[ones_bf, c8[c]], writes=[ph])
                s_ = sqb[(c * NTC + tc) % 2]
                b.op("scalar", lambda s, c=c, sl=sl, s_=s_: s.activation(
                    s_.t[:, 0:TCH], c8[c].t[:, sl], AF.Square),
                    reads=[c8[c]], writes=[s_])
                ph2, po2 = pos2[tc % 4]
                b.mm_group([lambda t, s_=s_, ph2=ph2, po2=po2: t.matmul(
                    ph2.t[:, po2:po2 + TCH], ones_bf.ap, s_.t[:, 0:TCH],
                    start=(c == 0), stop=(c == NCH - 1))],
                    reads=[ones_bf, s_], writes=[ph2])
        for tc in range(NTC):
            sl = slice(tc * TCH, (tc + 1) * TCH)
            ph, po = pos[tc % 4]
            ph2, po2 = pos2[tc % 4]
            b.op("vector", lambda v, sl=sl, ph=ph, po=po: v.tensor_scalar(
                ft.t[:, sl], ph.t[:, po:po + TCH], 1.0 / E, None,
                OP.mult), reads=[ph], writes=[ft])
            b.op("vector", lambda v, sl=sl: v.scalar_tensor_tensor(
                tmp[1].t[:, 0:TCH], ft.t[:, sl], -1.0,
                ft.t[:, sl], OP.mult, OP.mult),
                reads=[ft], writes=[tmp[1]])
            b.op("vector", lambda v: v.tensor_scalar(
                tmp[1].t[:, 0:TCH], tmp[1].t[:, 0:TCH], EPS, None, OP.add),
                reads=[tmp[1]], writes=[tmp[1]])
            b.op("vector", lambda v, sl=sl, ph2=ph2, po2=po2:
                 v.scalar_tensor_tensor(
                     aa.t[:, sl], ph2.t[:, po2:po2 + TCH], 1.0 / E,
                     tmp[1].t[:, 0:TCH], OP.mult, OP.add),
                 reads=[ph2, tmp[1]], writes=[aa])
        b.op("scalar", lambda s: s.activation(
            aa.ap, aa.ap, AF.Sqrt), reads=[aa], writes=[aa])
        b.op("vector", lambda v: v.reciprocal(aa.ap, aa.ap),
             reads=[aa], writes=[aa])
        b.op("vector", lambda v: v.scalar_tensor_tensor(
            bbt.ap, ft.ap, -1.0, aa.ap, OP.mult, OP.mult),
            reads=[ft, aa], writes=[bbt])

    def load_wslices(dst, dram_ap, col0, ncols, nin=E, ch="w", cast=True,
                     dst_off=0, ic0=0, nic_w=None):
        nic = nic_w if nic_w is not None else nin // 128
        t_ = dram_ap.tensor
        stride = t_.shape[-1]
        src = bass.AP(t_, col0 + ic0 * 128 * stride,
                      [[stride, 128], [128 * stride, nic], [1, ncols]])
        dfree = dst.t.shape[-1]
        dst_ap = bass.AP(dst.t, dst_off, [[dfree, 128], [ncols, nic],
                                          [1, ncols]])
        reads = ([dbu_wbt[t_.name[:-3]]] if t_.name.endswith("_bf")
                 else [])
        eng = {"wf0": "gpsimd", "wf1": "sync"}.get(ch, "sync")
        return b.dma(eng, dst_ap, src, ch=ch, reads=reads, writes=[dst])

    def part_scalar_reduce(src_col, dst_col, op, bcast=True, bcol=0):
        b.op("tensor", lambda t: t.transpose(
            pD.t[0:1, 0:128], amax_c.t[:, src_col:src_col + 1], id32.ap),
            reads=[amax_c, id32], writes=[pD])
        b.op("vector", lambda v: v.tensor_reduce(
            amax_c.t[0:1, dst_col:dst_col + 1], pD.t[0:1, 0:128], AX.X, op),
            reads=[pD], writes=[amax_c])
        if bcast:
            b.dma("sync", bounce_d[0:1, bcol:bcol + 1],
                  amax_c.t[0:1, dst_col:dst_col + 1], ch="bncw",
                  reads=[amax_c], writes=[dbu["bounce"]])
            b.dma("sync", amax_c.t[:, dst_col:dst_col + 1],
                  bounce_d[0:1, bcol:bcol + 1].broadcast_to((128, 1)),
                  ch="bncr", reads=[dbu["bounce"]], writes=[amax_c])

    def glob_prep(col_in, col_clip, col_recip, col_scale, k_scale, clip_lo):
        b.op("vector", lambda v: v.tensor_scalar(
            amax_c.t[:, col_clip:col_clip + 1],
            amax_c.t[:, col_in:col_in + 1], clip_lo, None, OP.max),
            reads=[amax_c], writes=[amax_c])
        b.op("vector", lambda v: v.reciprocal(
            amax_c.t[:, col_recip:col_recip + 1],
            amax_c.t[:, col_clip:col_clip + 1]),
            reads=[amax_c], writes=[amax_c])
        b.op("vector", lambda v: v.tensor_scalar(
            amax_c.t[:, col_scale:col_scale + 1],
            amax_c.t[:, col_recip:col_recip + 1], k_scale, None, OP.mult),
            reads=[amax_c], writes=[amax_c])

    def featmax_scales(m1, scl_o, rs_o):
        for tt in range(NT):
            b.op("tensor", lambda t, tt=tt: t.transpose(
                pD.t[:, 0:128], m1.t[:, tt * 128:(tt + 1) * 128], id32.ap),
                reads=[m1, id32], writes=[pD])
            b.op("vector", lambda v, tt=tt: v.tensor_reduce(
                tmp[0].t[:, tt:tt + 1], pD.t[:, 0:128], AX.X, OP.max),
                reads=[pD], writes=[tmp[0]])
        b.op("scalar", lambda s: s.activation(
            tmp[0].t[:, 0:NT], tmp[0].t[:, 0:NT], AF.Sqrt),
            reads=[tmp[0]], writes=[tmp[0]])
        b.op("vector", lambda v: v.tensor_scalar(
            tmp[0].t[:, 0:NT], tmp[0].t[:, 0:NT], 1e-5, None, OP.max),
            reads=[tmp[0]], writes=[tmp[0]])
        b.op("vector", lambda v: v.reciprocal(
            tmp[1].t[:, 0:NT], tmp[0].t[:, 0:NT]),
            reads=[tmp[0]], writes=[tmp[1]])
        b.op("vector", lambda v: v.tensor_scalar(
            tmp[1].t[:, 0:NT], tmp[1].t[:, 0:NT], 127.0, None, OP.mult),
            reads=[tmp[1]], writes=[tmp[1]])
        b.op("vector", lambda v: v.tensor_scalar(
            tmp[0].t[:, 0:NT], tmp[0].t[:, 0:NT], 1.0 / 127.0, None,
            OP.mult), reads=[tmp[0]], writes=[tmp[0]])
        b.op("vector", lambda v: v.tensor_copy(
            tmp[1].t[:, NT:2 * NT], tmp[0].t[:, 0:NT]),
            reads=[tmp[0]], writes=[tmp[1]])
        b.op("tensor", lambda t: t.transpose(
            pD.t[0:2 * NT, 0:128], tmp[1].t[:, 0:2 * NT], id32.ap),
            reads=[tmp[1], id32], writes=[pD])
        b.op("vector", lambda v: v.tensor_copy(
            tmp[0].t[0:2 * NT, 0:128], pD.t[0:2 * NT, 0:128]),
            reads=[pD], writes=[tmp[0]])
        b.dma("sync", stok_d[0:1, 0:T].rearrange("a (n x) -> (a n) x",
                                                   x=128),
              tmp[0].t[0:NT, 0:128], ch="stok", reads=[tmp[0]],
              writes=[dbu["stok"]])
        b.dma("sync", stok_d[1:2, 0:T].rearrange("a (n x) -> (a n) x",
                                                   x=128),
              tmp[0].t[NT:2 * NT, 0:128], ch="stok", reads=[tmp[0]],
              writes=[dbu["stok"]])
        b.dma("sync", scl_o.ap, stok_d[0:1, 0:T].broadcast_to((128, T)),
              ch="stok", reads=[dbu["stok"]], writes=[scl_o])
        b.dma("sync", rs_o.ap, stok_d[1:2, 0:T].broadcast_to((128, T)),
              ch="stok", reads=[dbu["stok"]], writes=[rs_o])

    a8 = [b.aalloc(f"a{c}", [128, T], F32) for c in range(NCH)]
    _xeng = ["sync", "scalar", "gpsimd"]
    for c in range(NCH):
        b.dma(_xeng[c % 3], a8[c].ap, x_in[c * 128:(c + 1) * 128, :],
              ch=f"x{c % 3}", writes=[a8[c]])
    emit_smalls()
    b.need("gpsimd", a8[NCH - 2].w)
    b.need("gpsimd", a8[NCH - 1].w)
    emit_precasts()
    for c in range(NCH):
        b.op("vector", lambda v, c=c: v.tensor_copy(c8[c].ap, a8[c].ap),
             reads=[a8[c]], writes=[c8[c]])
    ln_alloc()
    ln_stats_from_c8()
    for c in range(NCH):
        b.op("vector", lambda v, c=c: v.tensor_tensor(
            a8[c].ap, a8[c].ap, aa.ap, OP.mult),
            reads=[a8[c], aa], writes=[a8[c]])
        b.op("vector", lambda v, c=c: v.tensor_tensor(
            a8[c].ap, a8[c].ap, bbt.ap, OP.add),
            reads=[a8[c], bbt], writes=[a8[c]])
    ln_free()
    b.tap("xn0", a8)

    b.phase("amax8")
    for c in range(NCH):
        b.op("vector", lambda v, c=c: v.tensor_reduce(
            tmp[0].t[:, c:c + 1], a8[c].ap, AX.X, OP.max,
            apply_absolute_value=True), reads=[a8[c]], writes=[tmp[0]])
    b.op("vector", lambda v: v.tensor_reduce(
        amax_c.t[:, 0:1], tmp[0].t[:, 0:NCH], AX.X, OP.max),
        reads=[tmp[0]], writes=[amax_c])
    part_scalar_reduce(0, 1, OP.max, bcast=False)
    b.op("vector", lambda v: v.tensor_copy(
        amax_c.t[0:1, 24:32], amax_c.t[0:1, 1:2].broadcast_to((1, 8))),
        reads=[amax_c], writes=[amax_c])
    cc_w = b.dma("sync", cc_in_d.ap()[0:1, 0:8],
                 amax_c.t[0:1, 24:32], ch="cc", reads=[amax_c])
    b.need("gpsimd", cc_w)
    cc_tick = b._mark(nc.gpsimd.collective_compute(
        "AllReduce", OP.max, replica_groups=[list(range(n_cores))],
        ins=[cc_in_d.ap().opt()], outs=[cc_out_d.ap().opt()]), "mG", 1)
    ccbuf = Buf(None, "ccres")
    ccbuf.w = cc_tick
    b.phase("ln1")
    for c in range(NCH):
        b.op("vector", lambda v, c=c: v.tensor_copy(c8[c].ap, a8[c].ap),
             reads=[a8[c]], writes=[c8[c]])
    ln_alloc()
    ln_stats_from_c8()
    for c in range(NCH):
        b.op("vector", lambda v, c=c: v.tensor_tensor(
            ft.ap, a8[c].ap, aa.ap, OP.mult),
            reads=[a8[c], aa], writes=[ft])
        b.op("vector", lambda v, c=c: v.tensor_tensor(
            ft.ap, ft.ap, bbt.ap, OP.add), reads=[ft, bbt], writes=[ft])
        b.op("vector", lambda v, c=c: v.tensor_scalar(
            c8[c].ap, ft.ap, smalls["g1"].t[:, c:c + 1],
            smalls["b1"].t[:, c:c + 1], OP.mult, OP.add),
            reads=[ft, smalls["g1"], smalls["b1"]], writes=[c8[c]])
    ln_free()
    b.tap("xln1", c8)

    b.dma("sync", amax_c.t[:, 3:4],
          cc_out_d.ap()[0:1, 0:1].broadcast_to((128, 1)), ch="cc2",
          reads=[ccbuf], writes=[amax_c])
    glob_prep(3, 4, 5, 6, 224.0, 1e-12)
    q8s = b.aalloc("q8s", [128, T], FP8)
    for c in range(NCH):
        b.op("vector", lambda v, c=c: v.tensor_scalar(
            q8s.ap, a8[c].ap, amax_c.t[:, 6:7], None, OP.mult),
            reads=[a8[c], amax_c], writes=[q8s])
        b.dma("sync", xq8_spill[c * 128:(c + 1) * 128, :], q8s.ap,
              ch="spill", reads=[q8s], writes=[dbu["xq8"]])
    b.afree("q8s")

    for c in range(NCH):
        b.dma("sync", xb_spill[c * 128:(c + 1) * 128, :], a8[c].ap,
              ch="spill", reads=[a8[c]], writes=[dbu["xb"]])
    b.afree(*[f"a{c}" for c in range(NCH)])

    b.phase("Vproj")
    vt = [b.aalloc(f"vt{i}", [128, PW], BF16) for i in range(NT)]
    bvx = b.aalloc("bvx", [128, PW], F32)
    b.dma("sync", bvx.ap, DI["l0_bv_ext"][0:1, :].broadcast_to((128, PW)),
          ch="const", writes=[bvx])
    wvx = b.aalloc("wvx", [128, NCH * PW], BF16)
    load_wslices(wvx, WB["l0_vT_ext"], 0, PW, ch="wv")
    for tt in range(NT):
        fns = []
        for ic in range(NCH):
            for o0, ow in blocks(PW):
                ph, po = (pA, o0) if o0 < 1024 else (pB, o0 - 1024)
                fns.append(lambda t, ic=ic, o0=o0, ow=ow, tt=tt, ph=ph,
                           po=po: t.matmul(
                               ph.t[:, po:po + ow],
                               c8[ic].t[:, tt * 128:(tt + 1) * 128],
                               wvx.t[:, ic * PW + o0: ic * PW + o0 + ow],
                               start=(ic == 0), stop=(ic == NCH - 1)))
        b.mm_group(fns, reads=list(c8) + [wvx], writes=[pA, pB])
        w1_ = min(1024, PW)
        b.op("vector", lambda v, tt=tt, w1_=w1_: v.tensor_tensor(
            vt[tt].t[:, 0:w1_], pA.t[:, 0:w1_], bvx.t[:, 0:w1_], OP.add),
            reads=[pA, bvx], writes=[vt[tt]])
        if PW > 1024:
            b.op("vector", lambda v, tt=tt: v.tensor_tensor(
                vt[tt].t[:, 1024:PW], pB.t[:, 0:PW - 1024],
                bvx.t[:, 1024:PW], OP.add), reads=[pB, bvx],
                writes=[vt[tt]])
    b.afree("wvx")
    b.tap("vt0", [vt[0]])

    def attn_layer(layer, vtiles, vstride, extra=None):
        causal = layer == 0
        qkq = [b.aalloc(f"qkq{i}", [128, T], BF16) for i in range(2)]
        qkk = [b.aalloc(f"qkk{i}", [128, T], BF16) for i in range(2)]
        wqk = [b.aalloc(f"wqk{i}", [128, 2048], BF16) for i in range(2)]
        ptb = [[b.aalloc(f"pt{h}{i}", [128, QSP], BF16) for i in range(4)]
               for h in range(2)]
        stg = [b.aalloc(f"stg{h}", [128, QSP], BF16) for h in range(2)]
        stg2 = b.aalloc("stg2", [128, QSP], BF16)
        s64 = [b.aalloc(f"s64{h}", [128, QSP], F32) for h in range(2)]
        if causal:
            est = [[b.aalloc(f"est{j}{i}", [128, T], BF16)
                    for i in range(2)] for j in range(2)]
        negc = smalls["negc"]
        sS = [[(pA.t, 0, Buf(None, "sA0")), (pA.t, 512, Buf(None, "sA1"))],
              [(pB.t, 0, Buf(None, "sB0")), (pB.t, 512, Buf(None, "sB1"))]]
        cX = [(pC.t, 0, Buf(None, "cxA")), (pC.t, 512, Buf(None, "cxB"))]

        if layer == 1:
            load_wslices(wqk[0], WB["l1_WkT"], 0, HD, ch="wqk0")
            for th, tw in blocks(T):
                fns = [
                    (lambda t, ic=ic, th=th, tw=tw: t.matmul(
                        pD.t[0:64, 0:tw],
                        wqk[0].t[:, ic * HD: ic * HD + HD],
                        c8[ic].t[:, th:th + tw],
                        start=(ic == 0), stop=(ic == NCH - 1)))
                    for ic in range(NCH)]
                b.mm_group(fns, reads=list(c8) + [wqk[0]], writes=[pD])
                b.op("scalar", lambda s, th=th, tw=tw: s.activation(
                    qkk[0].t[0:64, th:th + tw], pD.t[0:64, 0:tw],
                    AF.Identity), reads=[pD], writes=[qkk[0]])
            b.dma("sync", qkk[0].t[64:128, 0:T], qkk[0].t[0:64, 0:T],
                  ch="ctxmv", reads=[qkk[0]], writes=[qkk[0]])

        def pair_tasks(p):
            par = p % 2
            hA, hB = 2 * p, 2 * p + 1
            tasks = []
            if layer == 0:
                tasks.append(lambda: load_wslices(
                    wqk[par], WB["l0_qkT"], p * 128, 128,
                    ch=f"wqk{par}"))
                tasks.append(lambda: load_wslices(
                    wqk[par], WB["l0_qkT"], E + p * 128, 128,
                    ch=f"wqk{par}", dst_off=1024))
                for jj, hh in ((0, hA), (1, hB)):
                    def ldslab(hh=hh, jj=jj):
                        b.dma("sync", est[jj][par].ap,
                              WB["l0_relstrip"][hh, :, :],
                              ch=f"estl{jj}",
                              reads=[dbu_wbt["l0_relstrip"]],
                              writes=[est[jj][par]])
                    tasks.append(ldslab)
            else:
                tasks.append(lambda: load_wslices(
                    wqk[par], WB["l1_WqT"], p * 128, 128, ch=f"wqk{par}"))

            def proj_block(dst, w_off, th, tw, bias_ap, bias_bufs,
                           scale_ap, scale_bufs):
                def go():
                    fns = [
                        (lambda t, ic=ic: t.matmul(
                            pD.t[:, 0:tw],
                            wqk[par].t[:, w_off + ic * 128:
                                       w_off + ic * 128 + 128],
                            c8[ic].t[:, th:th + tw],
                            start=(ic == 0), stop=(ic == NCH - 1)))
                        for ic in range(NCH)]
                    b.mm_group(fns, reads=list(c8) + [wqk[par]],
                               writes=[pD])
                    b.op("scalar", lambda s: s.activation(
                        dst.t[:, th:th + tw], pD.t[:, 0:tw],
                        AF.Identity, bias=bias_ap, scale=scale_ap),
                        reads=[pD] + bias_bufs + scale_bufs, writes=[dst])
                return go

            qsc = K(0.125)
            qbias = (smalls["bqks"].t[:, p:p + 1] if layer == 0
                     else smalls["bq1s"].t[:, p:p + 1])
            qbb = [smalls["bqks" if layer == 0 else "bq1s"]]
            for th, tw in blocks(T):
                tasks.append(proj_block(qkq[par], 0, th, tw, qbias, qbb,
                                        qsc, [b.kbuf]))
            if layer == 0:
                kbias = smalls["bqk"].t[:, 8 + p:9 + p]
                for th, tw in blocks(T):
                    tasks.append(proj_block(qkk[par], 1024, th, tw, kbias,
                                            [smalls["bqk"]], K(1.0),
                                            [b.kbuf]))

            def cbound():
                kk_ = qkk[par] if layer == 0 else qkk[0]
                ksc = -1.0
                for j in range(2):
                    r0 = 64 * j
                    b.mm_group([lambda t, r0=r0, kk_=kk_: t.matmul(
                        pD.t[:, 0:128], kk_.t[r0:r0 + 64, 0:128],
                        qkq[par].t[r0:r0 + 64, 0:128],
                        start=True, stop=True)],
                        reads=[qkq[par], kk_], writes=[pD])
                    b.op("vector", lambda v, j=j: v.tensor_reduce(
                        tmp[0].t[:, j:j + 1], pD.t[:, 0:128], AX.X,
                        OP.max), reads=[pD], writes=[tmp[0]])
                b.op("tensor", lambda t: t.transpose(
                    pD.t[0:2, 128:256], tmp[0].t[:, 0:2], id32.ap),
                    reads=[tmp[0], id32], writes=[pD])
                b.op("vector", lambda v: v.tensor_reduce(
                    tmp[0].t[0:2, 2:3], pD.t[0:2, 128:256], AX.X, OP.max),
                    reads=[pD], writes=[tmp[0]])
                b.op("vector", lambda v, ksc=ksc: v.tensor_scalar(
                    tmp[0].t[0:2, 3:4], tmp[0].t[0:2, 2:3], ksc, -8.0,
                    OP.mult, OP.add), reads=[tmp[0]], writes=[tmp[0]])
                b.dma("sync",
                      bass.AP(bounce_d.tensor, 10 + 2 * par, [[1, 2],
                                                              [1, 1]]),
                      tmp[0].t[0:2, 3:4], ch="bncw", reads=[tmp[0]],
                      writes=[dbu["bounce"]])
                b.dma("sync", negc.t[:, hA:hA + 2],
                      bass.AP(bounce_d.tensor, 10 + 2 * par,
                              [[0, 128], [1, 2]]),
                      ch="bncr", reads=[dbu["bounce"]],
                      writes=[negc])
            tasks.append(cbound)
            return tasks

        for t_ in pair_tasks(0):
            t_()
        expsc = K(1.0)

        for p in range(NCH):
            par = p % 2
            hA, hB = 2 * p, 2 * p + 1
            kk_ = qkk[par] if layer == 0 else qkk[0]
            tasks = pair_tasks(p + 1) if p + 1 < NCH else []
            for qs in range(NQS):
                qlo = qs * QSP
                qhi = qlo + QSP
                ktmax = (qhi - 1) // 128 if causal else NT - 1

                def rng(kt):
                    qstart = max(qlo, kt * 128) if causal else qlo
                    return qstart, qstart - qlo, qhi - qstart

                def emit_scores(kt):
                    qstart, qoff, width = rng(kt)
                    for j in range(2):
                        pt_, po, pbuf = sS[j][kt % 2]
                        r0 = 64 * j
                        fns = [
                            lambda t, r0=r0, kt=kt, pt_=pt_, po=po,
                            qoff=qoff, width=width, qstart=qstart:
                            t.matmul(
                                pt_[:, po + qoff:po + qoff + width],
                                kk_.t[r0:r0 + 64,
                                      kt * 128:(kt + 1) * 128],
                                qkq[par].t[r0:r0 + 64,
                                           qstart:qstart + width],
                                start=True, stop=not causal)]
                        reads = [qkq[par], kk_]
                        if causal:
                            eb = est[j][par]
                            e0 = qstart - kt * 128
                            fns.append(
                                lambda t, pt_=pt_, po=po, qoff=qoff,
                                width=width, eb=eb, e0=e0: t.matmul(
                                    pt_[:, po + qoff:po + qoff + width],
                                    id16.ap,
                                    eb.t[:, e0:e0 + width],
                                    start=False, stop=True))
                            reads += [eb, id16]
                        b.mm_group(fns, reads=reads, writes=[pbuf])

                def emit_exp(kt):
                    qstart, qoff, width = rng(kt)
                    for j, hh in ((0, hA), (1, hB)):
                        pt_, po, pbuf = sS[j][kt % 2]
                        ptx = ptb[j][kt % 4]
                        b.op("scalar", lambda s, pt_=pt_, po=po, ptx=ptx,
                             hh=hh, qoff=qoff, width=width: s.activation(
                                 ptx.t[:, qoff:qoff + width],
                                 pt_[:, po + qoff:po + qoff + width],
                                 AF.Exp, bias=negc.t[:, hh:hh + 1],
                                 scale=expsc),
                             reads=[pbuf, negc, b.kbuf], writes=[ptx])

                def emit_pv(kt):
                    qstart, qoff, width = rng(kt)
                    for j, hh in ((0, hA), (1, hB)):
                        ct_, co, cbuf = cX[j]
                        ptx = ptb[j][kt % 4]
                        v0 = hh * vstride
                        b.mm_group([
                            lambda t, kt=kt, ct_=ct_, co=co, ptx=ptx,
                            v0=v0, qoff=qoff, width=width: t.matmul(
                                ct_[0:65, co + qoff:co + qoff + width],
                                vtiles[kt].t[:, v0:v0 + 65],
                                ptx.t[:, qoff:qoff + width],
                                start=(kt == 0), stop=(kt == ktmax))],
                            reads=[vtiles[kt], ptx], writes=[cbuf])

                for step in range(ktmax + 4):
                    if step <= ktmax:
                        emit_scores(step)
                    if 1 <= step <= ktmax + 1:
                        emit_exp(step - 1)
                    if 3 <= step <= ktmax + 3:
                        emit_pv(step - 3)
                    if tasks and step <= ktmax:
                        tasks.pop(0)()
                    elif extra and step <= ktmax:
                        extra.pop(0)()

                for j, hh in ((0, hA), (1, hB)):
                    ct_, co, cbuf = cX[j]
                    b.op("vector", lambda v, ct_=ct_, co=co, j=j,
                         cbuf=cbuf, sgj=s64[j]: v.tensor_copy(
                             sgj.t[64:65, 0:QSP],
                             ct_[64:65, co:co + QSP]),
                         reads=[cbuf], writes=[s64[j]])
                    if hh % 2 == 0:
                        b.op("vector", lambda v, ct_=ct_, co=co,
                             qlo=qlo, qhi=qhi, p=p, cbuf=cbuf:
                             v.tensor_copy(
                                 ctx8[p].t[0:64, qlo:qhi],
                                 ct_[0:64, co:co + QSP]),
                             reads=[cbuf], writes=[ctx8[p]])
                    else:
                        b.op("vector", lambda v, ct_=ct_, co=co,
                             cbuf=cbuf: v.tensor_copy(
                                 stg2.t[0:64, 0:QSP],
                                 ct_[0:64, co:co + QSP]),
                             reads=[cbuf], writes=[stg2])
                        b.dma("sync", ctx8[p].t[64:128, qlo:qhi],
                              stg2.t[0:64, 0:QSP], ch="ctxmv",
                              reads=[stg2], writes=[ctx8[p]])
                    b.dma("sync", sig_d[hh:hh + 1, qlo:qhi],
                          s64[j].t[64:65, 0:QSP], ch="sigw",
                          reads=[s64[j]], writes=[dbu["sig0"]])
            for t_ in tasks:
                t_()
        while extra:
            extra.pop(0)()

        HT = H * T
        scols = HT // 128
        b.dma("sync", s64[0].t[:, 0:scols],
              bass.AP(sig_d.tensor, 0, [[scols, 128], [1, scols]]),
              ch="sigr", reads=[dbu["sig0"]], writes=[s64[0]])
        b.op("vector", lambda v: v.reciprocal(
            s64[0].t[:, 0:scols], s64[0].t[:, 0:scols]),
            reads=[s64[0]], writes=[s64[0]])
        b.op("vector", lambda v: v.tensor_copy(
            stg2.t[:, 0:scols], s64[0].t[:, 0:scols]),
            reads=[s64[0]], writes=[stg2])
        b.dma("sync", bass.AP(sig2_d.tensor, 0, [[scols, 128],
                                                 [1, scols]]),
              stg2.t[:, 0:scols], ch="sigw", reads=[stg2],
              writes=[dbu["sig1"]])
        for p in range(NCH):
            for qs in range(NQS):
                qlo = qs * QSP
                for j, hh in ((0, 2 * p), (1, 2 * p + 1)):
                    r0 = 64 * j
                    sgb = stg[j]
                    b.dma("sync", sgb.t[r0:r0 + 64, 0:QSP],
                          sig2_d[0:1, hh * T + qlo:
                                         hh * T + qlo + QSP]
                          .broadcast_to((64, QSP)),
                          ch="sigr", reads=[dbu["sig1"]], writes=[sgb])
                    b.op("vector", lambda v, p=p, r0=r0, qlo=qlo,
                         sgb=sgb: v.tensor_mul(
                             ctx8[p].t[r0:r0 + 64, qlo:qlo + QSP],
                             ctx8[p].t[r0:r0 + 64, qlo:qlo + QSP],
                             sgb.t[r0:r0 + 64, 0:QSP]),
                         reads=[ctx8[p], sgb], writes=[ctx8[p]])
        names = (["qkq0", "qkq1", "qkk0", "qkk1", "wqk0", "wqk1",
                  "stg0", "stg1", "stg2", "s640", "s641"] +
                 [f"pt{h}{i}" for h in range(2) for i in range(4)])
        if causal:
            names += ["est00", "est01", "est10", "est11"]
        b.afree(*names)

    b.phase("wmean")
    wfp2 = [b.aalloc("wf32b", [128, 1024], F32),
            b.aalloc("wf32c", [128, 1024], F32)]
    qwp = [b.aalloc("qw16a", [128, 1024], F16),
           b.aalloc("qw16b", [128, 1024], F16)]
    tqp = [b.aalloc("tq0", [128, 1024], F32),
           b.aalloc("tq1", [128, 1024], F32)]

    wm_tasks = []

    def wmean(dram_name, ncols_total, nin, col_lo):
        nblocks = ncols_total // 128
        nic = nin // 128
        k = 0
        for i in range(nblocks):
            for ic0 in range(0, nic, 8):
                def blk(k=k, i=i, ic0=ic0, dram_name=dram_name, nin=nin):
                    wf = wfp2[k % 2]
                    load_wslices(wf, DI[dram_name], i * 128, 128, nin=nin,
                                 ch="wf" + str(k % 2), cast=False,
                                 ic0=ic0, nic_w=8)
                    b.op("vector", lambda v, k=k, wf=wf: v.tensor_reduce(
                        tmp[1].t[:, k % 1024:k % 1024 + 1],
                        wf.t[:, 0:1024], AX.X, OP.add,
                        apply_absolute_value=True), reads=[wf],
                        writes=[tmp[1]])
                wm_tasks.append(blk)
                k += 1

        def fin(k=k, col_lo=col_lo, nin=nin, ncols_total=ncols_total):
            b.op("vector", lambda v: v.tensor_reduce(
                amax_c.t[:, col_lo:col_lo + 1], tmp[1].t[:, 0:k], AX.X,
                OP.add), reads=[tmp[1]], writes=[amax_c])
            part_scalar_reduce(col_lo, col_lo + 1, OP.add,
                               bcol=2 + col_lo % 8)
            b.op("vector", lambda v: v.tensor_scalar(
                amax_c.t[:, col_lo + 2:col_lo + 3],
                amax_c.t[:, col_lo + 1:col_lo + 2],
                1.0 / (nin * ncols_total), 1e-5, OP.mult, OP.max),
                reads=[amax_c], writes=[amax_c])
            b.op("vector", lambda v: v.reciprocal(
                amax_c.t[:, col_lo + 3:col_lo + 4],
                amax_c.t[:, col_lo + 2:col_lo + 3]),
                reads=[amax_c], writes=[amax_c])
        wm_tasks.append(fin)

    wmean("l1_w1T", HID, E, 16)
    wmean("l1_w2T", E, HID, 20)

    def wquant(wf, dst, n, sw_col, par):
        qw16 = qwp[par]
        tq = tqp[par]
        for o0, ow in blocks(n, 1024):
            b.op("vector", lambda v, o0=o0, ow=ow, wf=wf, qw16=qw16:
                 v.tensor_scalar(
                     qw16.t[:, 0:ow], wf.t[:, o0:o0 + ow],
                     amax_c.t[:, sw_col:sw_col + 1], 1536.0, OP.mult,
                     OP.add), reads=[wf, amax_c], writes=[qw16])
            b.op("vector", lambda v, o0=o0, ow=ow, qw16=qw16, tq=tq:
                 v.tensor_scalar(
                     tq.t[:, 0:ow], qw16.t[:, 0:ow], 1536.0, -1.0,
                     OP.subtract, OP.max), reads=[qw16], writes=[tq])
            b.op("vector", lambda v, o0=o0, ow=ow, tq=tq: v.tensor_scalar(
                dst.t[:, o0:o0 + ow], tq.t[:, 0:ow], 1.0, None, OP.min),
                reads=[tq], writes=[dst])

    b.phase("w1w2quant")
    w1q_spill = b.dram("w1q_spill", [NHC, 128, 1024], BF16).ap()
    dbu["w1q"] = Buf(None, "w1q")
    wqs = [b.aalloc("wqs0", [128, 1024], BF16),
           b.aalloc("wqs1", [128, 1024], BF16)]
    k = 0
    for hc in range(NHC):
        def q1blk(k=k, hc=hc):
            wf = wfp2[k % 2]
            wq_ = wqs[k % 2]
            load_wslices(wf, DI["l1_w1T"], hc * 128, 128,
                         ch="wf" + str(k % 2), cast=False)
            wquant(wf, wq_, 1024, 19, k % 2)
            b.dma("gpsimd", w1q_spill[hc, :, :], wq_.ap,
                  ch="w1qs", reads=[wq_], writes=[dbu["w1q"]])
        wm_tasks.append(q1blk)
        k += 1
    for oc in range(NCH):
        for ic0 in range(0, NHC, 8):
            def q2blk(k=k, oc=oc, ic0=ic0):
                wf = wfp2[k % 2]
                wq_ = wqs[k % 2]
                load_wslices(wf, DI["l1_w2T"], oc * 128, 128, nin=HID,
                             ch="wf" + str(k % 2), cast=False, ic0=ic0,
                             nic_w=8)
                wquant(wf, wq_, 1024, 23, k % 2)
                b.dma("sync",
                      w2q_spill[oc * 128:(oc + 1) * 128,
                                ic0 * 128:(ic0 + 8) * 128],
                      wq_.ap, ch="w2qs", reads=[wq_],
                      writes=[dbu["w2q"]])
            wm_tasks.append(q2blk)
            k += 1

    b.phase("L0attn")
    ctx8_alloc()
    attn_layer(0, vt, 65, extra=wm_tasks)
    b.afree("wf32b", "wf32c", "qw16a", "qw16b", "tq0", "tq1",
            "wqs0", "wqs1")
    b.afree(*[f"vt{i}" for i in range(NT)], "bvx")
    b.tap("ctx0", ctx8)

    def out_proj(w_name, bias_key, resid_dram):
        a8_ = [b.aalloc(f"a{c}", [128, T], F32) for c in range(NCH)]
        wo = [b.aalloc("wo0", [128, 1024], BF16),
              b.aalloc("wo1", [128, 1024], BF16)]
        for oc in range(NCH):
            wb_ = wo[oc % 2]
            load_wslices(wb_, WB[w_name], oc * 128, 128, ch="wo")
            if resid_dram is not None:
                b.dma("sync", a8_[oc].ap,
                      resid_dram[oc * 128:(oc + 1) * 128, :], ch="x",
                      reads=[dbu["xb"]], writes=[a8_[oc]])
            for th, tw in blocks(T):
                ph = pA if (th // 512) % 2 == 0 else pB
                fns = [
                    (lambda t, ic=ic, th=th, tw=tw, ph=ph, wb_=wb_:
                     t.matmul(
                         ph.t[:, 0:tw],
                         wb_.t[:, ic * 128:(ic + 1) * 128],
                         ctx8[ic].t[:, th:th + tw],
                         start=(ic == 0), stop=(ic == NCH - 1)))
                    for ic in range(NCH)]
                b.mm_group(fns, reads=list(ctx8) + [wb_], writes=[ph])
                if resid_dram is not None:
                    b.op("vector", lambda v, oc=oc, th=th, tw=tw, ph=ph:
                         v.scalar_tensor_tensor(
                             a8_[oc].t[:, th:th + tw], ph.t[:, 0:tw],
                             smalls[bias_key].t[:, oc:oc + 1],
                             a8_[oc].t[:, th:th + tw], OP.add, OP.add),
                         reads=[ph, smalls[bias_key], a8_[oc]],
                         writes=[a8_[oc]])
                else:
                    b.op("scalar", lambda s, oc=oc, th=th, tw=tw, ph=ph:
                         s.activation(
                             a8_[oc].t[:, th:th + tw], ph.t[:, 0:tw],
                             AF.Identity,
                             bias=smalls[bias_key].t[:, oc:oc + 1]),
                         reads=[ph, smalls[bias_key]], writes=[a8_[oc]])
        b.afree("wo0", "wo1")
        return a8_

    b.phase("L0outp")
    a8 = out_proj("l0_outT", "ob", xb_spill)
    ctx8_free()
    b.tap("xb1", a8)

    b.phase("ln2mlp")
    for c in range(NCH):
        b.op("vector", lambda v, c=c: v.tensor_copy(c8[c].ap, a8[c].ap),
             reads=[a8[c]], writes=[c8[c]])
    ln_alloc()
    ln_stats_from_c8()
    for c in range(NCH):
        b.op("vector", lambda v, c=c: v.tensor_tensor(
            ft.ap, a8[c].ap, aa.ap, OP.mult), reads=[a8[c], aa],
            writes=[ft])
        b.op("vector", lambda v, c=c: v.tensor_tensor(
            ft.ap, ft.ap, bbt.ap, OP.add), reads=[ft, bbt], writes=[ft])
        b.op("vector", lambda v, c=c: v.tensor_scalar(
            c8[c].ap, ft.ap, smalls["g2"].t[:, c:c + 1],
            smalls["b2"].t[:, c:c + 1], OP.mult, OP.add),
            reads=[ft, smalls["g2"], smalls["b2"]], writes=[c8[c]])
    ln_free()

    hbuf = [b.aalloc(f"h{i}", [128, THL], BF16) for i in range(NHC)]
    wm = [b.aalloc("wm0", [128, 1024], BF16),
          b.aalloc("wm1", [128, 1024], BF16)]
    for th0 in range(0, T, THL):
        for hc in range(NHC):
            wb_ = wm[hc % 2]
            load_wslices(wb_, WB["l0_w1T"], hc * 128, 128, ch="wmlp" + str(hc % 2))
            for q0, qw in blocks(THL):
                ph = pA if (q0 // 512) % 2 == 0 else pB
                fns = [
                    (lambda t, ic=ic, q0=q0, qw=qw, ph=ph, wb_=wb_:
                     t.matmul(
                         ph.t[:, 0:qw],
                         wb_.t[:, ic * 128:(ic + 1) * 128],
                         c8[ic].t[:, th0 + q0:th0 + q0 + qw],
                         start=(ic == 0), stop=(ic == NCH - 1)))
                    for ic in range(NCH)]
                b.mm_group(fns, reads=list(c8) + [wb_], writes=[ph])
                b.op("scalar", lambda s, hc=hc, q0=q0, qw=qw, ph=ph:
                     s.activation(
                         hbuf[hc].t[:, q0:q0 + qw], ph.t[:, 0:qw], AF.Relu,
                         bias=smalls["mb1"].t[:, hc:hc + 1]),
                     reads=[ph, smalls["mb1"]], writes=[hbuf[hc]])
        for oc in range(NCH):
            ph = pC if oc % 2 == 0 else pD
            for hcb in range(0, NHC, 8):
                wb_ = wm[(hcb // 8) % 2]
                t_ = WB["l0_w2T"].tensor
                stride = t_.shape[-1]
                srcp = bass.AP(t_, hcb * 128 * stride + oc * 128,
                               [[stride, 128], [128 * stride, 8], [1, 128]])
                dstp = bass.AP(wb_.t, 0, [[1024, 128], [128, 8], [1, 128]])
                b.dma("sync", dstp, srcp,
                      ch="wmlp" + str((hcb // 8) % 2),
                      reads=[dbu_wbt["l0_w2T"]], writes=[wb_])
                for q0, qw in blocks(THL):
                    fns = [
                        (lambda t, hh2=hh2, q0=q0, qw=qw, ph=ph, wb_=wb_,
                         hcb=hcb: t.matmul(
                             ph.t[:, q0:q0 + qw],
                             wb_.t[:, hh2 * 128:(hh2 + 1) * 128],
                             hbuf[hcb + hh2].t[:, q0:q0 + qw],
                             start=(hcb + hh2 == 0),
                             stop=(hcb + hh2 == NHC - 1)))
                        for hh2 in range(8)]
                    b.mm_group(fns, reads=list(hbuf) + [wb_], writes=[ph])
            for q0, qw in blocks(THL):
                b.op("vector", lambda v, oc=oc, q0=q0, qw=qw, ph=ph,
                     th0=th0: v.scalar_tensor_tensor(
                         a8[oc].t[:, th0 + q0:th0 + q0 + qw],
                         ph.t[:, q0:q0 + qw], smalls["mb2"].t[:, oc:oc + 1],
                         a8[oc].t[:, th0 + q0:th0 + q0 + qw],
                         OP.add, OP.add),
                     reads=[ph, smalls["mb2"], a8[oc]], writes=[a8[oc]])
    b.afree(*[f"h{i}" for i in range(NHC)], "wm0", "wm1")
    b.tap("s_l0", a8)

    b.phase("ffn8")
    wfp = [b.aalloc("wf32", [128, 1024], F32),
           b.aalloc("wf32x", [128, 1024], F32)]
    for oc in range(NCH):
        wf32 = wfp[oc % 2]
        load_wslices(wf32, DI["l0_ffnT"], oc * 128, 128,
                     ch="wf" + str(oc % 2), cast=False)
        b.op("vector", lambda v, oc=oc, wf32=wf32: v.tensor_reduce(
            tmp[0].t[:, oc:oc + 1], wf32.ap, AX.X, OP.max,
            apply_absolute_value=True), reads=[wf32], writes=[tmp[0]])
    b.op("vector", lambda v: v.tensor_reduce(
        amax_c.t[:, 8:9], tmp[0].t[:, 0:NCH], AX.X, OP.max),
        reads=[tmp[0]], writes=[amax_c])
    part_scalar_reduce(8, 9, OP.max, bcol=1)
    glob_prep(9, 10, 11, 12, 224.0, 1e-12)
    b.op("vector", lambda v: v.tensor_scalar(
        amax_c.t[:, 13:14], amax_c.t[:, 4:5], 1.0 / 224.0, None, OP.mult),
        reads=[amax_c], writes=[amax_c])
    b.op("vector", lambda v: v.tensor_scalar(
        amax_c.t[:, 14:15], amax_c.t[:, 10:11], 1.0 / 224.0, None,
        OP.mult), reads=[amax_c], writes=[amax_c])
    b.op("vector", lambda v: v.tensor_tensor(
        amax_c.t[:, 15:16], amax_c.t[:, 13:14], amax_c.t[:, 14:15],
        OP.mult), reads=[amax_c], writes=[amax_c])
    w8a = b.aalloc("w8a", [128, NCH * 1024], FP8)
    for oc in range(NCH):
        wf32 = wfp[oc % 2]
        load_wslices(wf32, DI["l0_ffnT"], oc * 128, 128,
                     ch="wf" + str(oc % 2), cast=False)
        b.op("vector", lambda v, oc=oc, wf32=wf32: v.tensor_scalar(
            w8a.t[:, oc * 1024:(oc + 1) * 1024], wf32.ap,
            amax_c.t[:, 12:13], None, OP.mult),
            reads=[wf32, amax_c], writes=[w8a])
    b.afree("wf32", "wf32x")
    xq8 = [b.aalloc(f"xq8_{c}", [128, T], FP8) for c in range(NCH)]
    for c in range(NCH):
        b.dma("sync", xq8[c].ap, xq8_spill[c * 128:(c + 1) * 128, :],
              ch="q1", reads=[dbu["xq8"]], writes=[xq8[c]])
    gtb = [b.aalloc(f"gt{i}", [128, 512], F32) for i in range(2)]
    xgb = [b.aalloc(f"xg{i}", [128, 512], F32) for i in range(2)]
    for oc in range(NCH):
        for th, tw in blocks(T):
            ph = pA if (th // 512) % 2 == 0 else pB
            gt = gtb[(th // 512) % 2]
            xg = xgb[(th // 512) % 2]
            fns = [
                (lambda t, ic=ic, th=th, tw=tw, ph=ph, oc=oc: t.matmul(
                    ph.t[:, 0:tw],
                    w8a.t[:, oc * 1024 + ic * 128:
                          oc * 1024 + (ic + 1) * 128],
                    xq8[ic].t[:, th:th + tw],
                    start=(ic == 0), stop=(ic == NCH - 1)))
                for ic in range(NCH)]
            b.mm_group(fns, reads=xq8 + [w8a], writes=[ph])
            b.op("scalar", lambda s, oc=oc, tw=tw, ph=ph, gt=gt:
                 s.activation(
                     gt.t[:, 0:tw], ph.t[:, 0:tw], AF.Silu,
                     bias=smalls["fb"].t[:, oc:oc + 1],
                     scale=amax_c.t[:, 15:16]),
                 reads=[ph, smalls["fb"], amax_c], writes=[gt])
            b.dma("scalar", xg.t[:, 0:tw],
                  x_in[oc * 128:(oc + 1) * 128, th:th + tw], ch="xgg",
                  writes=[xg])
            b.op("vector", lambda v, oc=oc, th=th, tw=tw, gt=gt:
                 v.tensor_mul(
                     gt.t[:, 0:tw], gt.t[:, 0:tw],
                     a8[oc].t[:, th:th + tw]),
                 reads=[gt, a8[oc]], writes=[gt])
            b.op("vector", lambda v, tw=tw, gt=gt, xg=xg: v.tensor_tensor(
                xg.t[:, 0:tw], xg.t[:, 0:tw], gt.t[:, 0:tw], OP.add),
                reads=[gt, xg], writes=[xg])
            b.dma("sync", x1_spill[oc * 128:(oc + 1) * 128, th:th + tw],
                  xg.t[:, 0:tw], ch="spill", reads=[xg],
                  writes=[dbu["x1"]])
    b.afree("w8a", "gt0", "gt1", "xg0", "xg1",
            *[f"xq8_{c}" for c in range(NCH)])
    b.afree(*[f"a{c}" for c in range(NCH)])

    b.phase("L1ln")
    xn1 = [b.aalloc(f"n{c}", [128, T], F32) for c in range(NCH)]
    for c in range(NCH):
        b.dma("sync", xn1[c].ap, x1_spill[c * 128:(c + 1) * 128, :],
              ch="x", reads=[dbu["x1"]], writes=[xn1[c]])
        b.op("vector", lambda v, c=c: v.tensor_copy(c8[c].ap, xn1[c].ap),
             reads=[xn1[c]], writes=[c8[c]])
    ln_alloc()
    ln_stats_from_c8()
    for c in range(NCH):
        b.op("vector", lambda v, c=c: v.tensor_tensor(
            xn1[c].ap, xn1[c].ap, aa.ap, OP.mult),
            reads=[xn1[c], aa], writes=[xn1[c]])
        b.op("vector", lambda v, c=c: v.tensor_tensor(
            xn1[c].ap, xn1[c].ap, bbt.ap, OP.add),
            reads=[xn1[c], bbt], writes=[xn1[c]])
        b.op("vector", lambda v, c=c: v.tensor_copy(c8[c].ap, xn1[c].ap),
             reads=[xn1[c]], writes=[c8[c]])
    ln_free()
    b.tap("xn1", xn1)

    b.phase("q1")
    rowm = b.aalloc("rowm", [128, T], F32)
    scl = b.aalloc("scl", [128, T], F32)
    rs1 = b.aalloc("rs1", [128, T], F32)
    qf16 = b.aalloc("qf16", [128, T], F16)
    b.op("scalar", lambda s: s.activation(
        rowm.ap, xn1[0].ap, AF.Square), reads=[xn1[0]], writes=[rowm])
    for c in range(1, NCH):
        b.op("scalar", lambda s, c=c: s.activation(
            ft.ap, xn1[c].ap, AF.Square), reads=[xn1[c]], writes=[ft])
        b.op("vector", lambda v: v.tensor_max(rowm.ap, rowm.ap, ft.ap),
             reads=[rowm, ft], writes=[rowm])
    featmax_scales(rowm, scl, rs1)
    for c in range(NCH):
        b.op("vector", lambda v, c=c: v.tensor_mul(
            ft.ap, xn1[c].ap, scl.ap), reads=[xn1[c], scl], writes=[ft])
        b.op("vector", lambda v: v.tensor_scalar(
            qf16.ap, ft.ap, 1536.0, None, OP.add), reads=[ft],
            writes=[qf16])
        b.op("vector", lambda v: v.tensor_scalar(
            ft.ap, qf16.ap, 1536.0, None, OP.subtract),
            reads=[qf16], writes=[ft])
        b.dma("gpsimd", xq1_spill[c * 128:(c + 1) * 128, :], ft.ap,
              ch="q1s", reads=[ft], writes=[dbu["xq1"]])
    b.afree("rowm", "qf16", *[f"n{c}" for c in range(NCH)])

    b.phase("V1")
    vt1 = [b.aalloc(f"u{i}", [128, 66], BF16) for i in range(NT)]
    bv1x = b.aalloc("bv1x", [128, 66], F32)
    b.dma("sync", bv1x.t[:, 0:65],
          DI["l1_bv_ext"][0:1, :].broadcast_to((128, HD + 1)),
          ch="const", writes=[bv1x])
    wv1 = b.aalloc("wv1", [128, NCH * 65], BF16)
    load_wslices(wv1, WB["l1_vT_ext"], 0, HD + 1, ch="wv")
    for tt in range(NT):
        fns = [
            (lambda t, ic=ic, tt=tt: t.matmul(
                pA.t[:, 0:65],
                c8[ic].t[:, tt * 128:(tt + 1) * 128],
                wv1.t[:, ic * 65:(ic + 1) * 65],
                start=(ic == 0), stop=(ic == NCH - 1)))
            for ic in range(NCH)]
        b.mm_group(fns, reads=list(c8) + [wv1], writes=[pA])
        b.op("vector", lambda v, tt=tt: v.tensor_tensor(
            vt1[tt].t[:, 0:65], pA.t[:, 0:65], bv1x.t[:, 0:65], OP.add),
            reads=[pA, bv1x], writes=[vt1[tt]])
    b.afree("wv1")

    b.phase("L1attn")
    ctx8_alloc()
    attn_layer(1, vt1, 0)
    b.afree(*[f"u{i}" for i in range(NT)], "bv1x")
    b.tap("ctx1", ctx8)

    b.phase("L1outp")
    a8 = out_proj("l1_WoT", "bo1", None)
    b.tap("s_l1", a8)

    for c in range(NCH):
        b.dma("sync", c8[c].ap, xq1_spill[c * 128:(c + 1) * 128, :],
              ch="q1", reads=[dbu["xq1"]], writes=[c8[c]])

    b.op("vector", lambda v: v.tensor_scalar(
        rs1.ap, rs1.ap, amax_c.t[:, 18:19], None, OP.mult),
        reads=[rs1, amax_c], writes=[rs1])
    b.phase("passA")
    rowm2 = b.aalloc("rowm2", [128, T], F32)
    b.op("vector", lambda v: v.memset(rowm2.ap, 0.0), writes=[rowm2])
    h1g = [b.aalloc("h1g0", [128, T + 2], BF16)] * 2
    h2f = [b.aalloc(f"h2f{i}", [128, 512], F32) for i in range(2)]
    sqf = [b.aalloc("sqf0", [128, 512], F32)] * 2
    w1qb = [b.aalloc("w1qb0", [128, 1024], BF16),
            b.aalloc("w1qb1", [128, 1024], BF16)]
    cdg = [b.aalloc("cdg0", [128, 3 * 128], BF16),
           b.aalloc("cdg1", [128, 3 * 128], BF16)]
    for hc in range(NHC):
        par = hc % 2
        wq_ = w1qb[par]
        hg = h1g[par]
        cd = cdg[par]
        b.dma("sync", wq_.ap, w1q_spill[hc, :, :], ch="w1ql" + str(par),
              reads=[dbu["w1q"]], writes=[wq_])
        b.dma("sync", cd.t[:, 0:3 * 128],
              WB["l1_convdiag"][hc, :, :], ch="cdg" + str(par),
              reads=[dbu_wbt["l1_convdiag"]], writes=[cd])
        for th, tw in blocks(T):
            ph = pA if (th // 512 + hc) % 2 == 0 else pB
            fns = [
                (lambda t, ic=ic, th=th, tw=tw, ph=ph, wq_=wq_: t.matmul(
                    ph.t[:, 0:tw],
                    wq_.t[:, ic * 128:(ic + 1) * 128],
                    c8[ic].t[:, th:th + tw],
                    start=(ic == 0), stop=(ic == NCH - 1)))
                for ic in range(NCH)]
            b.mm_group(fns, reads=list(c8) + [wq_], writes=[ph])
            b.op("vector", lambda v, th=th, tw=tw, ph=ph, hg=hg:
                 v.tensor_mul(
                     hg.t[:, 1 + th:1 + th + tw], ph.t[:, 0:tw],
                     rs1.t[:, th:th + tw]), reads=[ph, rs1], writes=[hg])
        b.op("vector", lambda v, hg=hg: v.memset(hg.t[:, 0:1], 0.0),
             writes=[hg])
        b.op("vector", lambda v, hg=hg: v.memset(hg.t[:, T + 1:T + 2],
                                                 0.0), writes=[hg])
        for th, tw in blocks(T):
            ph = pC if (th // 512 + hc) % 2 == 0 else pD
            h2 = h2f[(th // 512) % 2]
            sq = sqf[(th // 512) % 2]
            fns = [
                (lambda t, j=j, th=th, tw=tw, ph=ph, hg=hg, cd=cd:
                 t.matmul(
                     ph.t[:, 0:tw],
                     cd.t[:, j * 128:(j + 1) * 128],
                     hg.t[:, th + j:th + j + tw],
                     start=(j == 0), stop=(j == 2)))
                for j in range(3)]
            b.mm_group(fns, reads=[cd, hg], writes=[ph])
            b.op("scalar", lambda s, hc=hc, tw=tw, ph=ph, h2=h2:
                 s.activation(
                     h2.t[:, 0:tw], ph.t[:, 0:tw], AF.Silu,
                     bias=smalls["cb"].t[:, hc:hc + 1]),
                 reads=[ph, smalls["cb"]], writes=[h2])
            b.op("scalar", lambda s, tw=tw, h2=h2, sq=sq: s.activation(
                sq.t[:, 0:tw], h2.t[:, 0:tw], AF.Square),
                reads=[h2], writes=[sq])
            b.op("vector", lambda v, th=th, tw=tw, sq=sq: v.tensor_max(
                rowm2.t[:, th:th + tw], rowm2.t[:, th:th + tw],
                sq.t[:, 0:tw]), reads=[rowm2, sq], writes=[rowm2])
            b.dma("scalar" if (th // 512) % 2 == 0 else "gpsimd",
                  h2_spill[hc * 128:(hc + 1) * 128, th:th + tw],
                  h2.t[:, 0:tw], ch="h2s" + str((th // 512) % 2),
                  reads=[h2], writes=[dbu["h2"]])
    b.afree("h1g0", "h2f0", "h2f1", "sqf0",
            "w1qb0", "w1qb1", "cdg0", "cdg1")

    scl2 = scl
    rs2 = rs1
    featmax_scales(rowm2, scl2, rs2)
    b.afree("rowm2")

    b.phase("passB")
    qf16b = [b.aalloc(f"qf{i}", [128, 512], F16) for i in range(2)]
    ftb = [b.aalloc(f"ftb{i}", [128, 512], F32) for i in range(2)]
    xg2 = [b.aalloc(f"xg2{i}", [128, 512], F32) for i in range(2)]
    gt2 = [b.aalloc(f"gt2{i}", [128, 512], F32) for i in range(2)]
    h2c = [b.aalloc(f"h2c{i}", [128, 512], F32) for i in range(2)]
    w2s = [b.aalloc("w2s0", [128, NHC * 128], BF16),
           b.aalloc("w2s1", [128, NHC * 128], BF16)]
    for tc in range(NTC):
        th = tc * TCH
        tw = TCH
        for hc in range(NHC):
            par = hc % 2
            dstb = ctx8[hc % NCH]
            doff = (hc // NCH) * TCH
            b.dma("sync", h2c[par].t[:, 0:tw],
                  h2_spill[hc * 128:(hc + 1) * 128, th:th + tw],
                  ch="h2l" + str(par), reads=[dbu["h2"]],
                  writes=[h2c[par]])
            b.op("vector", lambda v, th=th, tw=tw, par=par: v.tensor_mul(
                ftb[par].t[:, 0:tw], h2c[par].t[:, 0:tw],
                scl2.t[:, th:th + tw]),
                reads=[h2c[par], scl2], writes=[ftb[par]])
            b.op("vector", lambda v, tw=tw, par=par: v.tensor_scalar(
                qf16b[par].t[:, 0:tw], ftb[par].t[:, 0:tw], 1536.0, None,
                OP.add), reads=[ftb[par]], writes=[qf16b[par]])
            b.op("vector", lambda v, dstb=dstb, doff=doff, tw=tw, par=par:
                 v.tensor_scalar(
                     dstb.t[:, doff:doff + tw], qf16b[par].t[:, 0:tw],
                     1536.0, None, OP.subtract),
                 reads=[qf16b[par]], writes=[dstb])
        for oc in range(NCH):
            par = oc % 2
            wsb = w2s[par]
            b.dma("gpsimd", wsb.ap,
                  w2q_spill[oc * 128:(oc + 1) * 128, :],
                  ch="w2l" + str(par), reads=[dbu["w2q"]], writes=[wsb])
            ph = pA if par == 0 else pB
            fns = [
                (lambda t, hc=hc, ph=ph, wsb=wsb, tw=tw: t.matmul(
                    ph.t[:, 0:tw],
                    wsb.t[:, hc * 128:(hc + 1) * 128],
                    ctx8[hc % NCH].t[:, (hc // NCH) * TCH:
                                     (hc // NCH) * TCH + tw],
                    start=(hc == 0), stop=(hc == NHC - 1)))
                for hc in range(NHC)]
            b.mm_group(fns, reads=list(ctx8) + [wsb], writes=[ph])
            b.op("vector", lambda v, th=th, tw=tw, ph=ph, par=par:
                 v.tensor_mul(
                     gt2[par].t[:, 0:tw], ph.t[:, 0:tw],
                     rs2.t[:, th:th + tw]),
                 reads=[ph, rs2], writes=[gt2[par]])
            b.op("scalar", lambda s, tw=tw, par=par: s.activation(
                gt2[par].t[:, 0:tw], gt2[par].t[:, 0:tw], AF.Silu,
                scale=amax_c.t[:, 22:23]),
                reads=[gt2[par], amax_c], writes=[gt2[par]])
            b.dma("sync", xg2[par].t[:, 0:tw],
                  x1_spill[oc * 128:(oc + 1) * 128, th:th + tw],
                  ch="xg2" + str(par), reads=[dbu["x1"]],
                  writes=[xg2[par]])
            b.op("vector", lambda v, oc=oc, tw=tw, par=par, th=th:
                 v.tensor_mul(
                     gt2[par].t[:, 0:tw], gt2[par].t[:, 0:tw],
                     a8[oc].t[:, th:th + tw]), reads=[gt2[par], a8[oc]],
                 writes=[gt2[par]])
            b.op("vector", lambda v, tw=tw, par=par: v.tensor_tensor(
                xg2[par].t[:, 0:tw], xg2[par].t[:, 0:tw],
                gt2[par].t[:, 0:tw], OP.add),
                reads=[gt2[par], xg2[par]], writes=[xg2[par]])
            b.dma("scalar", out_d[oc * 128:(oc + 1) * 128, th:th + tw],
                  xg2[par].t[:, 0:tw], ch="out" + str(par),
                  reads=[xg2[par]])

    for base in ("D:out0.", "D:out1."):
        for i in range(Bld._CH_DEPTH):
            och = base + str(i)
            if och in b.sems:
                b.eng["sync"].wait_ge(b.sems[och], b.cnt[och])
    return b

def _stage_weights(inp):
    w = {}
    tr = lambda a: np.ascontiguousarray(np.asarray(a, np.float32).T)
    col = lambda v: np.ascontiguousarray(
        np.asarray(v, np.float32).reshape(-1, 128).T)
    rel = np.asarray(inp["L0_attn_rel_pos"], np.float32)
    Tl = _T
    ki = np.arange(128)[:, None]
    fi = np.arange(Tl)[None, :]
    d = fi - ki
    strip = np.where((d >= 0) & (d < 2048),
                     rel[:, np.clip(d, 0, 2047)], np.float32(-1e9))
    w["l0_relstrip"] = np.ascontiguousarray(strip, np.float32)
    ipw = np.asarray(inp["L0_attn_in_proj_w"], np.float32)
    ipb = np.asarray(inp["L0_attn_in_proj_b"], np.float32)
    w["l0_qkT"] = np.ascontiguousarray(ipw[:2 * E].T)
    w["l0_bqk_c"] = col(ipb[:2 * E])
    wv = ipw[2 * E:]
    bv = ipb[2 * E:]
    vext = np.zeros((E, H * (HD + 1)), np.float32)
    bvext = np.zeros((1, H * (HD + 1)), np.float32)
    for h in range(H):
        vext[:, h * 65:h * 65 + 64] = wv[h * 64:(h + 1) * 64].T
        bvext[0, h * 65:h * 65 + 64] = bv[h * 64:(h + 1) * 64]
        bvext[0, h * 65 + 64] = 1.0
    w["l0_vT_ext"] = vext
    w["l0_bv_ext"] = bvext
    w["l0_outT"] = tr(inp["L0_attn_out_w"])
    w["l0_out_b_c"] = col(inp["L0_attn_out_b"])
    w["l0_w1T"] = tr(inp["L0_attn_w1"])
    w["l0_b1_c"] = col(inp["L0_attn_b1"])
    w["l0_w2T"] = tr(inp["L0_attn_w2"])
    w["l0_b2_c"] = col(inp["L0_attn_b2"])
    w["l0_ln1_g_c"] = col(inp["L0_attn_ln1_g"])
    w["l0_ln1_b_c"] = col(inp["L0_attn_ln1_b"])
    w["l0_ln2_g_c"] = col(inp["L0_attn_ln2_g"])
    w["l0_ln2_b_c"] = col(inp["L0_attn_ln2_b"])
    w["l0_ffnT"] = tr(inp["L0_ffn_weight"])
    w["l0_ffn_b_c"] = col(inp["L0_ffn_bias"])
    w["l1_WqT"] = tr(inp["L1_attn_Wq"])
    w["l1_bq_c"] = col(inp["L1_attn_bq"])
    w["l1_WkT"] = tr(inp["L1_attn_Wk"])
    wv1 = np.asarray(inp["L1_attn_Wv"], np.float32)
    bv1 = np.asarray(inp["L1_attn_bv"], np.float32)
    vext1 = np.zeros((E, HD + 1), np.float32)
    vext1[:, 0:HD] = wv1.T
    bvext1 = np.zeros((1, HD + 1), np.float32)
    bvext1[0, 0:HD] = bv1
    bvext1[0, HD] = 1.0
    w["l1_vT_ext"] = vext1
    w["l1_bv_ext"] = bvext1
    w["l1_WoT"] = tr(inp["L1_attn_Wo"])
    w["l1_bo_c"] = col(inp["L1_attn_bo"])
    w["l1_w1T"] = tr(inp["L1_ffn_w1"])
    w["l1_conv_b_c"] = col(inp["L1_ffn_conv_b"])
    w["l1_w2T"] = tr(inp["L1_ffn_w2"])
    cw = np.asarray(inp["L1_ffn_conv_w"], np.float32)[:, 0, :]
    diag = np.zeros((NHC, 3, 128, 128), np.float32)
    idx = np.arange(128)
    for hc in range(NHC):
        for j in range(3):
            diag[hc, j, idx, idx] = cw[hc * 128:(hc + 1) * 128, j]
    w["l1_convdiag"] = np.ascontiguousarray(
        diag.transpose(0, 2, 1, 3).reshape(NHC, 128, 3 * 128))
    return w


def kernel(**inputs):
    T = _T
    x = np.asarray(inputs["L0_attn_x"])
    B = x.shape[0]
    w = _stage_weights(inputs)
    bld = build_program(T=T, n_cores=B)
    bld.nc.finalize()
    in_maps = []
    for i in range(B):
        m = dict(w)
        m["xT"] = np.ascontiguousarray(x[i, :T, :].astype(np.float32).T)
        in_maps.append(m)
    res = run_bass_kernel_spmd(bld.nc, in_maps, core_ids=list(range(B)))
    outs = [np.ascontiguousarray(res.results[i]["outT"].T) for i in range(B)]
    return np.stack(outs).astype(np.float32)
